# version 07
# baseline (speedup 1.0000x reference)
import contextlib
import numpy as np

import concourse.bass as bass
import concourse.mybir as mybir
from concourse.bass_utils import run_bass_kernel_spmd

F32 = mybir.dt.float32
BF16 = mybir.dt.bfloat16
I32 = mybir.dt.int32
U8 = mybir.dt.uint8
AF = mybir.ActivationFunctionType
ALU = mybir.AluOpType
AX = mybir.AxisListType

SEQS = 2
S = 2048
D = 1024
DK = 8
H = 16
DH = 64
E = 8
FF = 4096
NCH = 16
NT4 = 4
NQT = 8
VSTRIDE = 65
SHIFT = 16.0

ENGS = ("sync", "tensor", "scalar", "vector", "gpsimd")


class Ctr:
    def __init__(self, handle):
        self.h = handle
        self.n = 0

    def bump(self, k):
        self.n += k
        return self.n


class Prog:

    def __init__(self):
        self.ops = {e: [] for e in ENGS}

    def op(self, eng, fn, waits=(), inc=None, inc_k=1):
        w = [(c.h, v) for (c, v) in waits if v > 0]
        val = None
        if inc is not None:
            val = inc.bump(inc_k)
        self.ops[eng].append((fn, w, (inc.h, inc_k) if inc is not None else None))
        return val

    def replay(self, blk):
        def mk(eng_name):
            ops = self.ops[eng_name]

            def run(e):
                guaranteed = {}
                for fn, waits, inc in ops:
                    for (h, v) in waits:
                        if guaranteed.get(id(h), -1) >= v:
                            continue
                        e.wait_ge(h, v)
                        guaranteed[id(h)] = v
                    ins = fn(e)
                    if inc is not None:
                        h, k = inc
                        assert ins is not None
                        ins.then_inc(h, k)
            return run

        blk.sync(mk("sync"))
        blk.tensor(mk("tensor"))
        blk.scalar(mk("scalar"))
        blk.vector(mk("vector"))
        blk.gpsimd(mk("gpsimd"))


class Builder:
    def __init__(self, nc, st):
        self.nc = nc
        self.p = Prog()
        self.st = st

        self.x = nc.dram_tensor("x", [SEQS, S, D], F32, kind="ExternalInput").ap()
        self.wqkp = nc.dram_tensor("wqkp", [D, 2048], F32, kind="ExternalInput").ap()
        self.bqkp = nc.dram_tensor("bqkp", [2048], F32, kind="ExternalInput").ap()
        self.wv = nc.dram_tensor("wv", [D, D], F32, kind="ExternalInput").ap()
        self.bv = nc.dram_tensor("bv", [D], F32, kind="ExternalInput").ap()
        self.wrt = nc.dram_tensor("wrt", [D, E], F32, kind="ExternalInput").ap()
        self.br = nc.dram_tensor("br", [E], F32, kind="ExternalInput").ap()
        self.w1t = nc.dram_tensor("w1t", [E, D, FF], F32, kind="ExternalInput").ap()
        self.b1p = nc.dram_tensor("b1p", [128, E * 32], F32, kind="ExternalInput").ap()
        self.w2t = nc.dram_tensor("w2t", [E, FF, D], F32, kind="ExternalInput").ap()
        self.b2 = nc.dram_tensor("b2", [E, D], F32, kind="ExternalInput").ap()
        self.gam = nc.dram_tensor("gam", [D], F32, kind="ExternalInput").ap()
        self.bet = nc.dram_tensor("bet", [D], F32, kind="ExternalInput").ap()
        self.out = nc.dram_tensor("out", [SEQS, S, D], F32, kind="ExternalOutput").ap()
        self.x1s = nc.dram_tensor("x1s", [SEQS, S, D], F32, kind="Internal").ap()
        self.qscr = nc.dram_tensor("qscr", [SEQS, 8, 128, S], BF16, kind="Internal").ap()

        sb = lambda name, shape, dt: st.enter_context(nc.sbuf_tensor(name, shape, dt))
        self.identf = sb("identf", [128, 128], F32)
        self.idi = sb("idi", [128, 128], I32)
        self.ones_bf = sb("ones_bf", [1, 512], BF16)
        self.ones_f = sb("ones_f", [1, 128], F32)
        self.bqk_bf = sb("bqk_bf", [1, 2048], BF16)
        self.bv_bf = sb("bv_bf", [1, D], BF16)
        self.b1_sb = sb("b1_sb", [128, E * 32], F32)
        self.b2_sb = sb("b2_sb", [E, D], F32)
        self.wrt_sb = sb("wrt_sb", [128, DK * E], F32)
        self.brb = sb("brb", [128, E], F32)
        self.gam_b = sb("gam_b", [128, D], F32)
        self.bet_b = sb("bet_b", [128, D], F32)
        self.router_bf = sb("router_bf", [E, S], BF16)
        self.b2_bf = sb("b2_bf", [E, D], BF16)
        self.er_sel_bf = sb("er_sel_bf", [E, E * 128], BF16)
        self.rbc = sb("rbc", [128, 2 * S], BF16)
        self.consts = sb("consts", [128, 8], F32)
        self.smalls = sb("smalls", [128, 32], F32)
        self.rout = sb("rout", [128, 40], F32)

        self.ARENA = 168448
        self.arena = st.enter_context(nc.sbuf_tensor("arena", [128, self.ARENA], U8))

        ps = lambda name: st.enter_context(nc.psum_tensor(name, [128, 512], F32))
        self.pA, self.pB = ps("pA"), ps("pB")
        self.pC, self.pD = ps("pC"), ps("pD")
        self.pE, self.pF = ps("pE"), ps("pF")

        sem = lambda name: st.enter_context(nc.semaphore(name))
        self.st_ = Ctr(sem("s_t"))
        self.sc_ = Ctr(sem("s_c"))
        self.sv_ = Ctr(sem("s_v"))
        self.sg_ = Ctr(sem("s_g"))
        self.qs_ = Ctr(sem("q_s"))
        self.qx_ = Ctr(sem("q_x"))
        self.qd_ = Ctr(sem("q_d"))
        self.qq_ = Ctr(sem("q_q"))
        self.qxd_ = Ctr(sem("q_xd"))
        self.qo_ = Ctr(sem("q_o"))
        self.qg_ = Ctr(sem("q_g"))
        self.qk0_ = Ctr(sem("q_k0"))
        self.qk1_ = Ctr(sem("q_k1"))
        self.wv0_ = Ctr(sem("q_v0"))
        self.wv1_ = Ctr(sem("q_v1"))
        self.qf_ = Ctr(sem("q_f"))
        self.qt0_ = Ctr(sem("q_t0"))
        self.qt1_ = Ctr(sem("q_t1"))

        self.o_kfm = 0
        self.o_vaug = 32768
        self.o_acc = 66048
        SCR = 131584
        self.SCR = SCR
        self.o_xt = 66048
        self.o_sq_p = 82432
        self.o_ln1bf = 86528
        self.o_win = 90624
        self.o_qwin = 107008
        self.o_wqst = 115200
        self.o_wqbf = 123392
        self.o_wvst = SCR
        self.o_wvbf = SCR + 16384
        self.o_pt = SCR
        self.o_attn = SCR + 8192
        self.o_xch = SCR + 16384
        self.o_tgts = SCR + 20480
        self.o_qqt = SCR + 24576
        self.o_sq_a = SCR + 32768
        self.o_w1b = 0
        self.o_w2b = 32768
        self.o_hblk = 49152
        self.o_wst = 57344
        self.o_tgtfm = SCR
        self.o_hrelu = SCR + 32768
        self.o_otm = 0
        self.o_xo = 4096
        self.o_tmp = 8192
        self.o_sq_o = 12288

    def carve(self, off, n, dt, parts=128):
        nb = n * mybir.dt.size(dt)
        assert off + nb <= self.ARENA, (off, nb)
        return self.arena[0:parts, off:off + nb].bitcast(dt)

    def mm(self, bank, lhsT, rhs, start, stop, waits=(), inc=None):
        return self.p.op(
            "tensor",
            lambda e, b=bank, l=lhsT, r=rhs, s0=start, s1=stop: e.matmul(
                b, l, r, start=s0, stop=s1),
            waits=waits, inc=inc)

    def setup(self):
        p = self.p
        v, sc, g, q = "vector", "scalar", "gpsimd", "sync"
        stage1 = self.carve(0, 2048, F32, parts=1)
        p.op(v, lambda e: e.memset(self.ones_bf[:], 1.0))
        p.op(v, lambda e: e.memset(self.ones_f[:], 1.0))
        p.op(v, lambda e: e.memset(self.consts[:, 0:1], -SHIFT))
        p.op(v, lambda e: e.memset(self.consts[:, 1:2], 1e-5), inc=self.sv_)
        ii = p.op(g, lambda e: e.iota(self.idi[:], [[1, 128]], base=0,
                                      channel_multiplier=-1), inc=self.sg_)
        i2 = p.op(v, lambda e: e.tensor_scalar(self.identf[:], self.idi[:], 0, None,
                                               op0=ALU.is_equal),
                  waits=[(self.sg_, ii)], inc=self.sv_)
        p.op(v, lambda e: e.tensor_copy(
            self.er_sel_bf[:].rearrange("p (j w) -> p j w", w=128),
            self.identf[0:E, 0:E].unsqueeze(2).broadcast_to((E, E, 128))),
            waits=[(self.sv_, i2)], inc=self.sv_)
        p.op(q, lambda e: e.dma_start(
            self.wrt_sb[:].rearrange("p (dk e) -> p dk e", e=E),
            self.wrt.rearrange("(dk p) e -> p dk e", p=128)), inc=self.qs_, inc_k=16)
        p.op(q, lambda e: e.dma_start(self.b1_sb[:], self.b1p), inc=self.qs_, inc_k=16)
        qb2 = p.op(q, lambda e: e.dma_start(self.b2_sb[:], self.b2),
                   inc=self.qs_, inc_k=16)
        p.op(v, lambda e: e.tensor_copy(self.b2_bf[:], self.b2_sb[:]),
             waits=[(self.qs_, qb2)], inc=self.sv_)
        q2 = p.op(q, lambda e: e.dma_start(
            stage1, self.bqkp.rearrange("(one n) -> one n", one=1)),
            inc=self.qs_, inc_k=16)
        c1 = p.op(v, lambda e: e.tensor_copy(self.bqk_bf[:], stage1),
                  waits=[(self.qs_, q2)], inc=self.sv_)
        q3 = p.op(q, lambda e: e.dma_start(
            stage1[0:1, 0:D], self.bv.rearrange("(one n) -> one n", one=1)),
            waits=[(self.sv_, c1)], inc=self.qs_, inc_k=16)
        c2 = p.op(v, lambda e: e.tensor_copy(self.bv_bf[:], stage1[0:1, 0:D]),
                  waits=[(self.qs_, q3)], inc=self.sv_)
        q4 = p.op(q, lambda e: e.dma_start(
            stage1[0:1, 0:D], self.gam.rearrange("(one n) -> one n", one=1)),
            waits=[(self.sv_, c2)], inc=self.qs_, inc_k=16)
        pE_busy = 0
        for half in range(2):
            t1 = self.mm(self.pE[:, 0:512], self.ones_f[0:1, 0:128],
                         stage1[0:1, half * 512: half * 512 + 512],
                         True, True, waits=[(self.qs_, q4),
                                            (self.sc_, pE_busy)], inc=self.st_)
            pE_busy = p.op(sc, lambda e, h=half: e.copy(
                self.gam_b[:, h * 512: h * 512 + 512], self.pE[:, 0:512]),
                waits=[(self.st_, t1)], inc=self.sc_)
        q5 = p.op(q, lambda e: e.dma_start(
            stage1[0:1, 0:D], self.bet.rearrange("(one n) -> one n", one=1)),
            waits=[(self.sc_, pE_busy)], inc=self.qs_, inc_k=16)
        for half in range(2):
            t1 = self.mm(self.pE[:, 0:512], self.ones_f[0:1, 0:128],
                         stage1[0:1, half * 512: half * 512 + 512],
                         True, True, waits=[(self.qs_, q5),
                                            (self.sc_, pE_busy)], inc=self.st_)
            pE_busy = p.op(sc, lambda e, h=half: e.copy(
                self.bet_b[:, h * 512: h * 512 + 512], self.pE[:, 0:512]),
                waits=[(self.st_, t1)], inc=self.sc_)
        q6 = p.op(q, lambda e: e.dma_start(
            stage1[0:1, 0:E], self.br.rearrange("(one n) -> one n", one=1)),
            waits=[(self.sc_, pE_busy)], inc=self.qs_, inc_k=16)
        t1 = self.mm(self.pE[:, 0:8], self.ones_f[0:1, 0:128],
                     stage1[0:1, 0:E], True, True,
                     waits=[(self.qs_, q6), (self.sc_, pE_busy)], inc=self.st_)
        p.op(sc, lambda e: e.copy(self.brb[:], self.pE[:, 0:8]),
             waits=[(self.st_, t1)], inc=self.sc_)

    def ln_stats(self, src_ap_fn, width, eng_waits, sq_off):
        p = self.p
        s1 = self.smalls[:, 0:1]
        s2 = self.smalls[:, 1:2]
        m = self.smalls[:, 2:3]
        msq = self.smalls[:, 3:4]
        var = self.smalls[:, 4:5]
        lnv = self.smalls[:, 5:6]
        rstd = self.smalls[:, 6:7]
        sq = self.carve(sq_off, 1024, F32)
        entry = [(self.sv_, self.sv_.n)]
        a1 = p.op("vector", lambda e: e.reduce_sum(s1, src_ap_fn(), axis=AX.X),
                  waits=list(eng_waits) + entry, inc=self.sv_)
        asq = p.op("vector", lambda e: e.tensor_tensor(
            sq[:, 0:width], src_ap_fn(), src_ap_fn(), op=ALU.mult),
            waits=list(eng_waits), inc=self.sv_)
        p.op("vector", lambda e: e.reduce_sum(s2, sq[:, 0:width], axis=AX.X),
             waits=[(self.sv_, asq)], inc=self.sv_)
        a2 = self.sv_.n
        a3 = p.op("vector", lambda e: e.tensor_scalar(
            m, s1, 1.0 / width, None, op0=ALU.mult),
            waits=[(self.sv_, a1)], inc=self.sv_)
        a4 = p.op("vector", lambda e: e.tensor_tensor(msq, m, m, op=ALU.mult),
                  waits=[(self.sv_, a3)], inc=self.sv_)
        a5 = p.op("vector", lambda e, w=width: e.scalar_tensor_tensor(
            var, s2, 1.0 / w, msq, op0=ALU.mult, op1=ALU.subtract),
            waits=[(self.sv_, a4), (self.sv_, a2)], inc=self.sv_)
        b1 = p.op("scalar", lambda e: e.activation(
            lnv, var, AF.Ln, bias=self.consts[:, 1:2], scale=1.0),
            waits=[(self.sv_, a5)], inc=self.sc_)
        b2 = p.op("scalar", lambda e: e.activation(rstd, lnv, AF.Exp, scale=-0.5),
                  waits=[(self.sc_, b1)], inc=self.sc_)
        return m, rstd, b2

    def stage_p(self, s):
        p = self.p
        kfm = self.carve(self.o_kfm, 8 * 2048, BF16)
        vaug = self.carve(self.o_vaug, 16 * 1040, BF16)
        von = p.op("vector", lambda e: e.memset(
            vaug.rearrange("p (c h z) -> p c h z", h=16, z=VSTRIDE)[:, :, :, 64:65],
            1.0), inc=self.sv_)
        self.vaug_ones = von

        lastln = 0
        win_reads = [0, 0]
        ln1_reads = [0, 0]
        qw_dump = 0
        wq_cast = {}
        last_w_mm = {}
        bankAB_evac = [self.sc_.n, self.sc_.n]
        bankCD_evac = [self.sc_.n, self.sc_.n]
        self.kfm_done_sc = 0
        self.vaug_done_sc = 0
        self.qdump_qs = 0

        for t4 in range(NT4):
            par = t4 % 2
            xt = self.carve(self.o_xt, 4 * 1024, F32)
            qx = p.op("sync", lambda e, s=s, t4=t4, xt=xt: e.dma_start(
                xt.rearrange("p (c d) -> p c d", d=D),
                self.x[s, t4 * 512:(t4 + 1) * 512, :].rearrange(
                    "(c p) d -> p c d", p=128)),
                waits=[(self.sv_, lastln)], inc=self.qx_, inc_k=16)
            tr_done = 0
            for c in range(4):
                cpar = c % 2
                xt_c = lambda xt=xt, c=c: xt[:, c * 1024:(c + 1) * 1024]
                m, rstd, bdone = self.ln_stats(xt_c, D, [(self.qx_, qx)],
                                               self.o_sq_p)
                ln1bf = self.carve(self.o_ln1bf + cpar * 2048, 1024, BF16)
                a6 = p.op("vector", lambda e, xt_c=xt_c, m=m, rstd=rstd,
                          ln1bf=ln1bf: e.scalar_tensor_tensor(
                              ln1bf, xt_c(), m, rstd.broadcast_to((128, 1024)),
                              op0=ALU.subtract, op1=ALU.mult),
                          waits=[(self.sc_, bdone),
                                 ((self.qt0_, self.qt1_)[cpar], ln1_reads[cpar])],
                          inc=self.sv_)
                lastln = a6
                wdst = self.carve(self.o_win + par * 8192, 8 * 512, BF16)
                qtc = (self.qt0_, self.qt1_)[cpar]
                for dk in range(DK):
                    p.op("scalar", lambda e, wdst=wdst, dk=dk, c=c,
                         ln1bf=ln1bf: e.dma_start_transpose(
                             wdst[:, dk * 512 + c * 128: dk * 512 + c * 128 + 128],
                             ln1bf[:, dk * 128:(dk + 1) * 128]),
                         waits=[(self.sv_, a6), (self.st_, win_reads[par])],
                         inc=qtc, inc_k=16)
                ln1_reads[cpar] = qtc.n
            tr_w = [(self.qt0_, self.qt0_.n), (self.qt1_, self.qt1_.n)]
            win = self.carve(self.o_win + par * 8192, 8 * 512, BF16)
            qwin = self.carve(self.o_qwin, 8 * 512, BF16)
            for oc in range(16):
                po = oc % 2
                wqst = self.carve(self.o_wqst + po * 4096, 8 * 128, F32)
                wqbf = self.carve(self.o_wqbf + po * 2048, 8 * 128, BF16)
                qsem = (self.qk0_, self.qk1_)[po]
                qd = p.op("sync", lambda e, oc=oc, wqst=wqst: e.dma_start(
                    wqst.rearrange("p (dk o) -> p dk o", o=128),
                    self.wqkp[:, oc * 128:(oc + 1) * 128].rearrange(
                        "(dk p) o -> p dk o", p=128)),
                    waits=[(self.sv_, wq_cast.get(("qk", po), 0))],
                    inc=qsem, inc_k=16)
                cc = p.op("vector", lambda e, wqbf=wqbf, wqst=wqst: e.tensor_copy(
                    wqbf, wqst),
                    waits=[(qsem, qd), (self.st_, last_w_mm.get(("qk", po), 0))],
                    inc=self.sv_)
                wq_cast[("qk", po)] = cc
                bank = (self.pA, self.pB)[oc % 2]
                for dk in range(DK):
                    self.mm(bank[:, 0:512],
                            wqbf[:, dk * 128:(dk + 1) * 128],
                            win[:, dk * 512:(dk + 1) * 512],
                            start=(dk == 0), stop=False,
                            waits=(([(self.sv_, cc),
                                     (self.sc_, bankAB_evac[oc % 2])] + tr_w)
                                   if dk == 0 else ()))
                tmm = self.mm(bank[:, 0:512],
                              self.bqk_bf[0:1, oc * 128:(oc + 1) * 128],
                              self.ones_bf[0:1, 0:512], start=False, stop=True,
                              inc=self.st_)
                last_w_mm[("qk", po)] = tmm
                if oc < 8:
                    dst = qwin[:, oc * 512:(oc + 1) * 512]
                    w_extra = [(self.qd_, qw_dump)]
                else:
                    dst = kfm[:, (oc - 8) * 2048 + t4 * 512:
                              (oc - 8) * 2048 + t4 * 512 + 512]
                    w_extra = []
                ev = p.op("scalar", lambda e, dst=dst, bank=bank: e.activation(
                    dst, bank[:, 0:512], AF.Copy, scale=1.0 / 32.0),
                    waits=[(self.st_, tmm)] + w_extra, inc=self.sc_)
                bankAB_evac[oc % 2] = ev
                if oc == 15:
                    self.kfm_done_sc = ev
            qev = self.sc_.n
            qw_dump = p.op("sync", lambda e, s=s, t4=t4, qwin=qwin: e.dma_start(
                self.qscr[s, :, :, t4 * 512:(t4 + 1) * 512].rearrange(
                    "oc p t -> p oc t"),
                qwin.rearrange("p (oc t) -> p oc t", t=512)),
                waits=[(self.sc_, qev)], inc=self.qd_, inc_k=16)
            self.qdump_qs = qw_dump
            for vc in range(4):
                pv = vc % 2
                wvst = self.carve(self.o_wvst + pv * 8192, 8 * 256, F32)
                wvbf = self.carve(self.o_wvbf + pv * 4096, 8 * 256, BF16)
                vsem = (self.wv0_, self.wv1_)[pv]
                qd = p.op("sync", lambda e, vc=vc, wvst=wvst: e.dma_start(
                    wvst.rearrange("p (dk o) -> p dk o", o=256),
                    self.wv[:, vc * 256:(vc + 1) * 256].rearrange(
                        "(dk p) o -> p dk o", p=128)),
                    waits=[(self.sv_, wq_cast.get(("v", pv), 0))],
                    inc=vsem, inc_k=16)
                cc = p.op("vector", lambda e, wvbf=wvbf, wvst=wvst: e.tensor_copy(
                    wvbf, wvst),
                    waits=[(vsem, qd), (self.st_, last_w_mm.get(("v", pv), 0))],
                    inc=self.sv_)
                wq_cast[("v", pv)] = cc
                for c in range(4):
                    bpar = (vc * 4 + c) % 2
                    bank = (self.pC, self.pD)[bpar]
                    for dk in range(DK):
                        self.mm(bank[:, 0:256],
                                win[:, dk * 512 + c * 128: dk * 512 + c * 128 + 128],
                                wvbf[:, dk * 256:(dk + 1) * 256],
                                start=(dk == 0), stop=False,
                                waits=(([(self.sv_, cc),
                                         (self.sc_, bankCD_evac[bpar])] + tr_w)
                                       if dk == 0 else ()))
                    tmm = self.mm(bank[:, 0:256], self.ones_bf[0:1, 0:128],
                                  self.bv_bf[0:1, vc * 256:(vc + 1) * 256],
                                  start=False, stop=True, inc=self.st_)
                    last_w_mm[("v", pv)] = tmm
                    gch = t4 * 4 + c
                    dst = vaug[:, gch * 1040: (gch + 1) * 1040].rearrange(
                        "p (h z) -> p h z", z=VSTRIDE)[:, vc * 4:(vc + 1) * 4, 0:64]
                    ev = p.op("scalar", lambda e, dst=dst, bank=bank: e.activation(
                        dst, bank[:, 0:256].rearrange("p (h w) -> p h w", w=64),
                        AF.Copy, scale=1.0 / 32.0),
                        waits=[(self.st_, tmm), (self.sv_, von)], inc=self.sc_)
                    bankCD_evac[bpar] = ev
            win_reads[par] = self.st_.n
            self.vaug_done_sc = self.sc_.n
        self.stageP_end = dict(sv=self.sv_.n, sc=self.sc_.n, st=self.st_.n,
                               sg=self.sg_.n)

    def stage_a(self, s):
        p = self.p
        kfm = self.carve(self.o_kfm, 8 * 2048, BF16)
        vaug = self.carve(self.o_vaug, 16 * 1040, BF16)
        acc = self.carve(self.o_acc, 8 * 2048, F32)
        pt = self.carve(self.o_pt, 2 * 512, BF16)
        attn = self.carve(self.o_attn, 2 * 1024, F32)
        xch = self.carve(self.o_xch, 1024, F32)
        tgts = self.carve(self.o_tgts, 1024, F32)

        pt_war = [0, 0]
        entryA = [(self.st_, self.stageP_end["st"]),
                  (self.sg_, self.stageP_end["sg"]),
                  (self.sc_, self.stageP_end["sc"]),
                  (self.sv_, self.stageP_end["sv"])]
        qreload_war = 0
        qchain = 0
        bankAB_evac = [self.stageP_end["sc"], self.stageP_end["sc"]]
        bankCD_free = [self.stageP_end["sc"], self.stageP_end["sc"]]
        attn_war = [0, 0]
        xch_war = 0
        xch_war_qs = 0
        tgts_war = 0
        pv_stop = [0, 0]
        pEF_free = [0, 0]
        self.acc_sc = 0
        self.router_sc = 0
        rs0 = self.smalls[:, 8:9]
        rs1 = self.smalls[:, 9:10]

        for qt in range(NQT):
            qq = qt % 2
            qqt = self.carve(self.o_qqt + qq * 4096, 8 * 256, BF16)
            qr = p.op("sync", lambda e, s=s, qt=qt, qqt=qqt: e.dma_start(
                qqt.rearrange("p (oc t) -> p oc t", t=256),
                self.qscr[s, :, :, qt * 256:(qt + 1) * 256].rearrange(
                    "oc p t -> p oc t")),
                waits=[(self.qd_, self.qdump_qs), (self.st_, qreload_war),
                       (self.st_, qchain)],
                inc=self.qq_, inc_k=16)
            qt_scores_start = self.st_.n
            for h in range(H):
                hp = (h % 2) * 64
                hc = h // 2
                evs = {}

                def emit_pv(kp, h=h):
                    nonlocal pv_stop
                    ptb = pt[:, (kp % 2) * 512:(kp % 2) * 512 + 512]
                    ev = evs[kp]
                    for half in range(2):
                        kc = 2 * kp + half
                        for q2 in range(2):
                            bank = (self.pC, self.pD)[q2]
                            self.mm(
                                bank[:, 0:65],
                                ptb[:, half * 256 + q2 * 128:
                                    half * 256 + q2 * 128 + 128],
                                vaug[:, kc * 1040 + h * VSTRIDE:
                                     kc * 1040 + h * VSTRIDE + VSTRIDE],
                                start=(kc == 0), stop=(kc == NCH - 1),
                                waits=([(self.sc_, ev),
                                        (self.sc_, bankCD_free[q2]),
                                        (self.sc_, self.vaug_done_sc),
                                        (self.sv_, self.vaug_ones)]
                                       if kc < 2 else [(self.sc_, ev)]),
                                inc=(self.st_ if kc == NCH - 1 else None))
                    pt_war[kp % 2] = self.st_.n
                    if kp == 7:
                        pv_stop = [self.st_.n - 1, self.st_.n]

                for kp in range(8):
                    sbank = (self.pA, self.pB)[kp % 2]
                    ptb = pt[:, (kp % 2) * 512:(kp % 2) * 512 + 512]
                    for half in range(2):
                        kc = 2 * kp + half
                        tmm = self.mm(
                            sbank[:, half * 256: half * 256 + 256],
                            kfm[hp:hp + 64,
                                hc * 2048 + kc * 128: hc * 2048 + kc * 128 + 128],
                            qqt[hp:hp + 64, hc * 256: hc * 256 + 256],
                            start=True, stop=True,
                            waits=[(self.qq_, qr),
                                   (self.sc_, bankAB_evac[kp % 2]),
                                   (self.st_, pt_war[kp % 2]),
                                   (self.sc_, self.kfm_done_sc)],
                            inc=self.st_)
                        if h == 0 and kp == 0 and half == 0:
                            qchain = tmm
                    ev = p.op("scalar", lambda e, sbank=sbank, ptb=ptb:
                              e.activation(
                                  ptb, sbank[:, 0:512],
                                  AF.Exp, bias=self.consts[:, 0:1],
                                  scale=1.0 / 64.0),
                              waits=[(self.st_, tmm)] + entryA, inc=self.sc_)
                    bankAB_evac[kp % 2] = ev
                    evs[kp] = ev
                    if kp >= 1:
                        emit_pv(kp - 1)
                emit_pv(7)
                for q2 in range(2):
                    bank = (self.pC, self.pD)[q2]
                    tpv = pv_stop[q2]
                    rs = (rs0, rs1)[q2]
                    rv = p.op("vector", lambda e, bank=bank, rs=rs: e.reciprocal(
                        rs, bank[:, 64:65]),
                        waits=[(self.st_, tpv)], inc=self.sv_)
                    ev = p.op("scalar", lambda e, bank=bank, rs=rs, h=h, q2=q2,
                              attn=attn: e.activation(
                                  attn[:, q2 * 1024 + h * 64:
                                       q2 * 1024 + h * 64 + 64],
                                  bank[:, 0:64], AF.Copy, scale=rs),
                              waits=[(self.sv_, rv), (self.sv_, attn_war[q2])]
                                    + entryA,
                              inc=self.sc_)
                    bankCD_free[q2] = ev
            qreload_war = qt_scores_start
            attn_done = self.sc_.n
            for q2 in range(2):
                g = qt * 2 + q2
                attn_c = lambda attn=attn, q2=q2: attn[:, q2 * 1024:(q2 + 1) * 1024]
                m, rstd, bdone = self.ln_stats(attn_c, D, [(self.sc_, attn_done)],
                                               self.o_sq_a)
                qx = p.op("sync", lambda e, s=s, g=g, xch=xch: e.dma_start(
                    xch, self.x[s, g * 128:(g + 1) * 128, :]),
                    waits=[(self.sv_, xch_war), (self.qxd_, xch_war_qs)]
                          + entryA,
                    inc=self.qx_, inc_k=16)
                a1 = p.op("vector", lambda e, attn_c=attn_c, m=m, rstd=rstd,
                          tgts=tgts: e.scalar_tensor_tensor(
                              tgts, attn_c(), m, rstd.broadcast_to((128, D)),
                              op0=ALU.subtract, op1=ALU.mult),
                          waits=[(self.sc_, bdone), (self.st_, tgts_war)]
                                + entryA,
                          inc=self.sv_)
                attn_war[q2] = a1
                a2 = p.op("vector", lambda e, xch=xch, tgts=tgts: e.tensor_tensor(
                    xch, xch, tgts, op=ALU.add),
                    waits=[(self.sv_, a1), (self.qx_, qx)], inc=self.sv_)
                xch_f = lambda xch=xch: xch[:, 0:D]
                m2, rstd2, b2d = self.ln_stats(xch_f, D, [(self.sv_, a2)],
                                               self.o_sq_a)
                a3 = p.op("vector", lambda e, xch=xch, m2=m2, rstd2=rstd2,
                          tgts=tgts: e.scalar_tensor_tensor(
                              tgts, xch, m2, rstd2.broadcast_to((128, D)),
                              op0=ALU.subtract, op1=ALU.mult),
                          waits=[(self.sc_, b2d)], inc=self.sv_)
                qd = p.op("sync", lambda e, s=s, g=g, xch=xch: e.dma_start(
                    self.x1s[s, g * 128:(g + 1) * 128, :], xch),
                    waits=[(self.sv_, a2)], inc=self.qxd_, inc_k=16)
                xch_war = a3
                xch_war_qs = qd
                for dk in range(DK):
                    bank = (self.pE, self.pF)[dk % 2]
                    t1 = p.op("tensor", lambda e, bank=bank, tgts=tgts, dk=dk:
                              e.transpose(bank[:, 0:128],
                                          tgts[:, dk * 128:(dk + 1) * 128],
                                          self.identf[:]),
                              waits=[(self.sv_, a3), (self.sc_, pEF_free[dk % 2])],
                              inc=self.st_)
                    cp = p.op("scalar", lambda e, bank=bank, dk=dk, g=g, acc=acc:
                              e.copy(acc[:, dk * 2048 + g * 128:
                                         dk * 2048 + g * 128 + 128],
                                     bank[:, 0:128]),
                              waits=[(self.st_, t1)], inc=self.sc_)
                    pEF_free[dk % 2] = cp
                tgts_war = self.st_.n
                acc_done = self.sc_.n
                self.acc_sc = acc_done
                for dk in range(DK):
                    self.mm(self.pF[:, 0:8],
                            acc[:, dk * 2048 + g * 128: dk * 2048 + g * 128 + 128],
                            self.wrt_sb[:, dk * 8:(dk + 1) * 8],
                            start=(dk == 0), stop=(dk == DK - 1),
                            waits=([(self.sc_, acc_done)] if dk == 0 else ()),
                            inc=(self.st_ if dk == DK - 1 else None))
                tlg = self.st_.n
                self.emit_router_chain(g, tlg)
                pEF_free[1] = max(pEF_free[1], self.router_pf_free)
        self.x1dump_qxd = self.qxd_.n
        self.stageA_end = dict(sv=self.sv_.n, sc=self.sc_.n, st=self.st_.n,
                               sg=self.sg_.n)

    def emit_router_chain(self, g, tlg):
        p = self.p
        lg = self.rout[:, 0:8]
        d = self.rout[:, 8:16]
        mask = self.rout[:, 16:24]
        d2 = self.rout[:, 24:32]
        w = self.rout[:, 32:40]
        m1 = self.smalls[:, 16:17]
        m2 = self.smalls[:, 17:18]
        e2 = self.smalls[:, 18:19]
        sS = self.smalls[:, 19:20]
        rs = self.smalls[:, 20:21]
        entry = [(self.sv_, self.sv_.n)]
        a1 = p.op("vector", lambda e: e.tensor_tensor(lg, self.pF[:, 0:8],
                                                      self.brb[:], op=ALU.add),
                  waits=[(self.st_, tlg)] + entry, inc=self.sv_)
        a2 = p.op("vector", lambda e: e.reduce_max(m1, lg, axis=AX.X),
                  waits=[(self.sv_, a1)], inc=self.sv_)
        a3 = p.op("vector", lambda e: e.tensor_scalar(d, lg, m1, None,
                                                      op0=ALU.subtract),
                  waits=[(self.sv_, a2)], inc=self.sv_)
        a4 = p.op("vector", lambda e: e.tensor_scalar(mask, d, 0.0, None,
                                                      op0=ALU.is_equal),
                  waits=[(self.sv_, a3)], inc=self.sv_)
        a5 = p.op("vector", lambda e: e.scalar_tensor_tensor(
            d2, mask, -1e9, d, op0=ALU.mult, op1=ALU.add),
            waits=[(self.sv_, a4)], inc=self.sv_)
        a6 = p.op("vector", lambda e: e.reduce_max(m2, d2, axis=AX.X),
                  waits=[(self.sv_, a5)], inc=self.sv_)
        b1 = p.op("scalar", lambda e: e.activation(e2, m2, AF.Exp, scale=1.0),
                  waits=[(self.sv_, a6)], inc=self.sc_)
        a7 = p.op("vector", lambda e: e.tensor_scalar(sS, e2, 1.0, None,
                                                      op0=ALU.add),
                  waits=[(self.sc_, b1)], inc=self.sv_)
        a8 = p.op("vector", lambda e: e.reciprocal(rs, sS),
                  waits=[(self.sv_, a7)], inc=self.sv_)
        a9 = p.op("vector", lambda e: e.tensor_scalar(mask, d, m2, None,
                                                      op0=ALU.is_ge),
                  waits=[(self.sv_, a6)], inc=self.sv_)
        b2 = p.op("scalar", lambda e: e.activation(lg, d, AF.Exp, scale=1.0),
                  waits=[(self.sv_, a9)], inc=self.sc_)
        a10 = p.op("vector", lambda e: e.scalar_tensor_tensor(
            w, lg, rs, mask, op0=ALU.mult, op1=ALU.mult),
            waits=[(self.sc_, b2), (self.sv_, a8)], inc=self.sv_)
        t1 = p.op("tensor", lambda e: e.transpose(
            self.pF[0:8, 128:256], w, self.identf[:]),
            waits=[(self.sv_, a10)], inc=self.st_)
        c1 = p.op("scalar", lambda e, g=g: e.copy(
            self.router_bf[0:8, g * 128:(g + 1) * 128], self.pF[0:8, 128:256]),
            waits=[(self.st_, t1)], inc=self.sc_)
        self.router_pf_free = c1
        self.router_sc = c1

    def stage_f(self, s):
        p = self.p
        acc = self.carve(self.o_acc, 8 * 2048, F32)
        tgtfm = self.carve(self.o_tgtfm, 8 * 2048, BF16)
        hblk = self.carve(self.o_hblk, 8 * 512, BF16)
        A_end = self.stageA_end
        kv_free = [(self.st_, A_end["st"]), (self.sc_, A_end["sc"]),
                   (self.sv_, A_end["sv"]), (self.qxd_, self.x1dump_qxd),
                   (self.qx_, self.qx_.n), (self.qq_, self.qq_.n)]

        cast_done = 0
        for dk in range(DK):
            cast_done = p.op("vector", lambda e, dk=dk, acc=acc, tgtfm=tgtfm:
                             e.tensor_copy(tgtfm[:, dk * 2048:(dk + 1) * 2048],
                                           acc[:, dk * 2048:(dk + 1) * 2048]),
                             waits=[(self.sc_, self.acc_sc)] + kv_free,
                             inc=self.sv_)
        self.tgtfm_sv = cast_done

        pEF_free = [A_end["sc"], A_end["sc"]]
        pEF_free_v = [0, 0]
        for tt in range(4):
            for dk in range(DK):
                bp = dk % 2
                bank = (self.pE, self.pF)[bp]
                tmm = self.mm(bank[:, 0:512],
                              self.b2_bf[0:8, dk * 128:(dk + 1) * 128],
                              self.router_bf[0:8, tt * 512:(tt + 1) * 512],
                              start=True, stop=True,
                              waits=[(self.sc_, self.router_sc),
                                     (self.sv_, pEF_free_v[bp]),
                                     (self.sv_, cast_done)],
                              inc=self.st_)
                av = p.op("vector", lambda e, bank=bank, dk=dk, tt=tt, acc=acc:
                          e.tensor_tensor(
                              acc[:, dk * 2048 + tt * 512:
                                  dk * 2048 + tt * 512 + 512],
                              bank[:, 0:512],
                              acc[:, dk * 2048 + tt * 512:
                                  dk * 2048 + tt * 512 + 512],
                              op=ALU.add),
                          waits=[(self.st_, tmm)], inc=self.sv_)
                pEF_free_v[bp] = av

        wstA = self.carve(self.o_wst, 1024, F32)
        wstB = self.carve(self.o_wst + 4096, 1024, F32)
        lane_buf = [wstA, wstB]
        lane_q = [self.qf_, self.qg_]
        lane_eng = ["sync", "gpsimd"]
        lane_cast = [0, 0]

        st_box = dict(w1_mm_war=[A_end["st"], A_end["st"]],
                      w2_mm_war=A_end["st"])
        rbc_sc = [0, 0]
        rbc_war_sv = [0, 0]
        bankAB_evac = [A_end["sc"], A_end["sc"]]
        bankCD_free = [max(pEF_free_v), max(pEF_free_v)]
        hrelu_war = [0, 0]
        hblk_war = [A_end["st"]]
        pE_free_sc = [A_end["sc"]]
        pE_free_v = max(pEF_free_v)
        h_last_sv = [0]

        blocks = [(e_, fb) for e_ in range(E) for fb in range(4)]
        NB = len(blocks)
        w1c = [0] * NB
        w2c = [0] * NB

        w1_qd = {}
        w2_qd = {}

        def w1_load(i, dk):
            e_, fb = blocks[i]
            ln = dk % 2
            buf, qsem = lane_buf[ln], lane_q[ln]
            w1_qd[(i, dk)] = p.op(
                lane_eng[ln], lambda e, e_=e_, fb=fb, buf=buf:
                e.dma_start(buf, self.w1t[e_, dk * 128:(dk + 1) * 128,
                                          fb * 1024:(fb + 1) * 1024]),
                waits=[(self.sv_, lane_cast[ln])] + kv_free,
                inc=qsem, inc_k=16)

        def w1_cast(i, dk):
            ln = dk % 2
            buf, qsem = lane_buf[ln], lane_q[ln]
            w1b = self.carve(self.o_w1b + (i % 2) * 16384, 8 * 1024, BF16)
            cc = p.op("vector", lambda e, dk=dk, buf=buf, w1b=w1b:
                      e.tensor_copy(w1b[:, dk * 1024:(dk + 1) * 1024], buf),
                      waits=[(qsem, w1_qd[(i, dk)]),
                             (self.st_, st_box["w1_mm_war"][i % 2])],
                      inc=self.sv_)
            lane_cast[ln] = cc
            w1c[i] = max(w1c[i], cc)

        def w2_load(i, fc):
            e_, fb = blocks[i]
            ln = fc % 2
            buf, qsem = lane_buf[ln], lane_q[ln]
            w2_qd[(i, fc)] = p.op(
                lane_eng[ln], lambda e, e_=e_, fb=fb, buf=buf:
                e.dma_start(buf, self.w2t[e_, fb * 1024 + fc * 128:
                                          fb * 1024 + (fc + 1) * 128, :]),
                waits=[(self.sv_, lane_cast[ln])] + kv_free,
                inc=qsem, inc_k=16)

        def w2_cast(i, fc):
            ln = fc % 2
            buf, qsem = lane_buf[ln], lane_q[ln]
            w2b = self.carve(self.o_w2b, 8 * 1024, BF16)
            cc = p.op("vector", lambda e, fc=fc, buf=buf, w2b=w2b:
                      e.tensor_copy(w2b[:, fc * 1024:(fc + 1) * 1024], buf),
                      waits=[(qsem, w2_qd[(i, fc)]),
                             (self.st_, st_box["w2_mm_war"])],
                      inc=self.sv_)
            lane_cast[ln] = cc
            w2c[i] = max(w2c[i], cc)

        def emit_rbc(e_):
            ep = e_ % 2
            for tt in range(4):
                tmm = self.mm(self.pE[:, 0:512],
                              self.er_sel_bf[0:E, e_ * 128:(e_ + 1) * 128],
                              self.router_bf[0:E, tt * 512:(tt + 1) * 512],
                              start=True, stop=True,
                              waits=[(self.sc_, self.router_sc),
                                     (self.sv_, pE_free_v),
                                     (self.sc_, pE_free_sc[0]),
                                     (self.sv_, rbc_war_sv[ep])],
                              inc=self.st_)
                cv = p.op("scalar", lambda e, tt=tt, ep=ep: e.copy(
                    self.rbc[:, ep * 2048 + tt * 512: ep * 2048 + tt * 512 + 512],
                    self.pE[:, 0:512]),
                    waits=[(self.st_, tmm)], inc=self.sc_)
                pE_free_sc[0] = cv
            rbc_sc[ep] = pE_free_sc[0]

        def emit_compute(i, slot_ops):
            e_, fb = blocks[i]
            ep = e_ % 2
            wpar = i % 2
            w1b = self.carve(self.o_w1b + wpar * 16384, 8 * 1024, BF16)
            w2b = self.carve(self.o_w2b, 8 * 1024, BF16)
            slot = 0
            for tt in range(4):
                for fc in range(8):
                    for f in slot_ops.get(slot, ()):
                        f()
                    slot += 1
                    bp = fc % 2
                    bank = (self.pA, self.pB)[bp]
                    for dk in range(DK):
                        self.mm(bank[:, 0:512],
                                w1b[:, dk * 1024 + fc * 128:
                                    dk * 1024 + fc * 128 + 128],
                                tgtfm[:, dk * 2048 + tt * 512:
                                      dk * 2048 + tt * 512 + 512],
                                start=(dk == 0), stop=(dk == DK - 1),
                                waits=([(self.sv_, w1c[i]),
                                        (self.sv_, self.tgtfm_sv),
                                        (self.sc_, bankAB_evac[bp])]
                                       if dk == 0 else ()),
                                inc=(self.st_ if dk == DK - 1 else None))
                    tmm = self.st_.n
                    hre = self.carve(self.o_hrelu + (fc % 2) * 1024, 512, BF16)
                    ev = p.op("scalar", lambda e, bank=bank, e_=e_, fb=fb,
                              fc=fc, hre=hre: e.activation(
                                  hre, bank[:, 0:512], AF.Relu,
                                  bias=self.b1_sb[:, e_ * 32 + fb * 8 + fc:
                                                  e_ * 32 + fb * 8 + fc + 1],
                                  scale=1.0),
                              waits=[(self.st_, tmm),
                                     (self.sv_, hrelu_war[fc % 2])]
                                    + kv_free,
                              inc=self.sc_)
                    bankAB_evac[bp] = ev
                    hm = p.op("vector", lambda e, hre=hre, fc=fc, tt=tt,
                              ep=ep, hblk=hblk: e.tensor_tensor(
                                  hblk[:, fc * 512:(fc + 1) * 512], hre,
                                  self.rbc[:, ep * 2048 + tt * 512:
                                           ep * 2048 + tt * 512 + 512],
                                  op=ALU.mult),
                              waits=[(self.sc_, ev), (self.sc_, rbc_sc[ep]),
                                     (self.st_, hblk_war[0])],
                              inc=self.sv_)
                    hrelu_war[fc % 2] = hm
                    h_last_sv[0] = hm
                st_box["w1_mm_war"][wpar] = self.st_.n
                for dk in range(DK):
                    bp = dk % 2
                    bank = (self.pC, self.pD)[bp]
                    for fc in range(8):
                        self.mm(bank[:, 0:512],
                                w2b[:, fc * 1024 + dk * 128:
                                    fc * 1024 + dk * 128 + 128],
                                hblk[:, fc * 512:(fc + 1) * 512],
                                start=(fc == 0), stop=(fc == 7),
                                waits=([(self.sv_, h_last_sv[0]),
                                        (self.sv_, w2c[i]),
                                        (self.sv_, bankCD_free[bp])]
                                       if fc == 0 else ()),
                                inc=(self.st_ if fc == 7 else None))
                    tmm = self.st_.n
                    av = p.op("vector", lambda e, bank=bank, dk=dk, tt=tt,
                              acc=acc: e.tensor_tensor(
                                  acc[:, dk * 2048 + tt * 512:
                                      dk * 2048 + tt * 512 + 512],
                                  bank[:, 0:512],
                                  acc[:, dk * 2048 + tt * 512:
                                      dk * 2048 + tt * 512 + 512],
                                  op=ALU.add),
                              waits=[(self.st_, tmm)], inc=self.sv_)
                    bankCD_free[bp] = av
                st_box["w2_mm_war"] = self.st_.n
                hblk_war[0] = self.st_.n
            if fb == 3:
                rbc_war_sv[ep] = h_last_sv[0]

        for dk in range(DK):
            w1_load(0, dk)
            w1_cast(0, dk)
        for fc in range(DK):
            w2_load(0, fc)
            w2_cast(0, fc)
        for i in range(NB):
            e_, fb = blocks[i]
            slot_ops = {}
            if i > 0:
                for fc in range(DK):
                    slot_ops.setdefault(fc, []).append(
                        lambda fc=fc, i=i: w2_cast(i, fc))
                    if fc + 2 < DK:
                        slot_ops.setdefault(fc, []).append(
                            lambda fc=fc, i=i: w2_load(i, fc + 2))
            if i + 1 < NB:
                for dk in range(DK):
                    slot_ops.setdefault(10 + dk, []).insert(0, (
                        lambda dk=dk, i=i: w1_cast(i + 1, dk)))
                    slot_ops.setdefault(8 + dk, []).append(
                        lambda dk=dk, i=i: w1_load(i + 1, dk))
                for fc in range(2):
                    slot_ops.setdefault(18 + fc, []).append(
                        lambda fc=fc, i=i: w2_load(i + 1, fc))
            if fb == 0:
                emit_rbc(e_)
            emit_compute(i, slot_ops)
        self.stageF_end = dict(sv=self.sv_.n, sc=self.sc_.n, st=self.st_.n,
                               sg=self.sg_.n)

    def stage_o(self, s):
        p = self.p
        acc = self.carve(self.o_acc, 8 * 2048, F32)
        otm = self.carve(self.o_otm, 1024, F32)
        xo = self.carve(self.o_xo, 1024, F32)
        tmp = self.carve(self.o_tmp, 1024, F32)
        F_end = self.stageF_end
        pEF_free = [F_end["sc"], F_end["sc"]]
        acc_ready = [(self.sv_, F_end["sv"]), (self.sg_, F_end["sg"])]
        otm_war = 0
        otm_war_qs = 0
        xo_war = 0
        for g in range(NCH):
            cps = 0
            for dk in range(DK):
                bank = (self.pE, self.pF)[dk % 2]
                t1 = p.op("tensor", lambda e, bank=bank, dk=dk, g=g, acc=acc:
                          e.transpose(bank[:, 0:128],
                                      acc[:, dk * 2048 + g * 128:
                                          dk * 2048 + g * 128 + 128],
                                      self.identf[:]),
                          waits=acc_ready + [(self.sc_, pEF_free[dk % 2]),
                                             (self.sv_, otm_war),
                                             (self.qo_, otm_war_qs)],
                          inc=self.st_)
                cp = p.op("scalar", lambda e, bank=bank, dk=dk, otm=otm:
                          e.copy(otm[:, dk * 128:(dk + 1) * 128], bank[:, 0:128]),
                          waits=[(self.st_, t1)], inc=self.sc_)
                pEF_free[dk % 2] = cp
                cps = cp
            otm_f = lambda otm=otm: otm[:, 0:D]
            m, rstd, bdone = self.ln_stats(otm_f, D, [(self.sc_, cps)],
                                           self.o_sq_o)
            a1 = p.op("vector", lambda e, otm=otm, m=m, rstd=rstd, tmp=tmp:
                      e.scalar_tensor_tensor(
                          tmp, otm, m, rstd.broadcast_to((128, D)),
                          op0=ALU.subtract, op1=ALU.mult),
                      waits=[(self.sc_, bdone)], inc=self.sv_)
            a2 = p.op("vector", lambda e, tmp=tmp: e.tensor_tensor(
                tmp, tmp, self.gam_b[:], op=ALU.mult),
                waits=[(self.sv_, a1)], inc=self.sv_)
            a3 = p.op("vector", lambda e, tmp=tmp: e.tensor_tensor(
                tmp, tmp, self.bet_b[:], op=ALU.add),
                waits=[(self.sv_, a2)], inc=self.sv_)
            tmp_f = lambda tmp=tmp: tmp[:, 0:D]
            m3, rstd3, b3done = self.ln_stats(tmp_f, D, [(self.sv_, a3)],
                                              self.o_sq_o)
            qx = p.op("sync", lambda e, s=s, g=g, xo=xo: e.dma_start(
                xo, self.x1s[s, g * 128:(g + 1) * 128, :]),
                waits=[(self.sv_, xo_war), (self.qxd_, self.x1dump_qxd)],
                inc=self.qx_, inc_k=16)
            a4 = p.op("vector", lambda e, tmp=tmp, m3=m3, rstd3=rstd3, otm=otm:
                      e.scalar_tensor_tensor(
                          otm, tmp, m3, rstd3.broadcast_to((128, D)),
                          op0=ALU.subtract, op1=ALU.mult),
                      waits=[(self.sc_, b3done)], inc=self.sv_)
            a5 = p.op("vector", lambda e, otm=otm, xo=xo: e.tensor_tensor(
                otm, otm, xo, op=ALU.add),
                waits=[(self.sv_, a4), (self.qx_, qx)], inc=self.sv_)
            xo_war = a5
            otm_war_qs = p.op("sync", lambda e, s=s, g=g, otm=otm: e.dma_start(
                self.out[s, g * 128:(g + 1) * 128, :], otm),
                waits=[(self.sv_, a5)], inc=self.qo_, inc_k=16)
            otm_war = a5

    def seq_barrier(self):
        p = self.p
        marks = [(c, c.n) for c in (self.st_, self.sc_, self.sv_, self.sg_,
                 self.qs_, self.qx_, self.qd_, self.qq_, self.qxd_, self.qo_,
                 self.qk0_, self.qk1_, self.wv0_, self.wv1_, self.qf_,
                 self.qg_, self.qt0_, self.qt1_)]
        marks = [(c, v) for (c, v) in marks]
        for eng in ENGS:
            p.op(eng, lambda e: e.nop(), waits=marks)

    def finish(self):
        self.p.op("sync", lambda e: e.nop(), waits=[(self.qo_, self.qo_.n)])

    def build(self, stop_after=None, nseqs=SEQS):
        self.setup()
        self.seq_barrier()
        if stop_after == "S":
            self.finish()
            with self.nc.Block() as blk:
                self.p.replay(blk)
            return
        for s in range(nseqs):
            if s > 0:
                self.seq_barrier()
            self.stage_p(s)
            if stop_after == "P":
                break
            self.stage_a(s)
            if stop_after == "A":
                break
            self.stage_f(s)
            if stop_after == "F":
                break
            self.stage_o(s)
        self.finish()
        with self.nc.Block() as blk:
            self.p.replay(blk)


def build_program(stop_after=None, nseqs=SEQS):
    nc = bass.Bass("TRN2", target_bir_lowering=False)
    with contextlib.ExitStack() as st:
        b = Builder(nc, st)
        b.build(stop_after=stop_after, nseqs=nseqs)
    return nc


_NC_CACHE = None


def kernel(L0_attn_x, L0_attn_Wqk, L0_attn_bqk, L0_attn_Wv, L0_attn_bv,
           L0_ffn_Wr, L0_ffn_br, L0_ffn_W1, L0_ffn_b1, L0_ffn_W2, L0_ffn_b2,
           L0_ffn_gamma, L0_ffn_beta):
    global _NC_CACHE
    x = np.asarray(L0_attn_x, dtype=np.float32)
    Wqk = np.asarray(L0_attn_Wqk, np.float32).reshape(D, H, 2, DH)
    qpart = Wqk[:, :, 0, :].reshape(D, 8, 2 * DH)
    kpart = Wqk[:, :, 1, :].reshape(D, 8, 2 * DH)
    wqkp = np.ascontiguousarray(
        np.concatenate([qpart, kpart], axis=1).reshape(D, 2048))
    bqk = np.asarray(L0_attn_bqk, np.float32).reshape(H, 2, DH)
    bqkp = np.ascontiguousarray(np.concatenate(
        [bqk[:, 0, :].reshape(8, 128), bqk[:, 1, :].reshape(8, 128)],
        axis=0).reshape(2048))
    wv = np.ascontiguousarray(np.asarray(L0_attn_Wv, np.float32))
    bv = np.ascontiguousarray(np.asarray(L0_attn_bv, np.float32))
    wrt = np.ascontiguousarray(np.asarray(L0_ffn_Wr, np.float32).T)
    br = np.ascontiguousarray(np.asarray(L0_ffn_br, np.float32))
    w1t = np.ascontiguousarray(np.asarray(L0_ffn_W1, np.float32).transpose(0, 2, 1))
    b1p = np.ascontiguousarray(
        np.asarray(L0_ffn_b1, np.float32).reshape(E, 32, 128).transpose(2, 0, 1)
        .reshape(128, E * 32))
    w2t = np.ascontiguousarray(np.asarray(L0_ffn_W2, np.float32).transpose(0, 2, 1))
    b2 = np.ascontiguousarray(np.asarray(L0_ffn_b2, np.float32))
    gam = np.ascontiguousarray(np.asarray(L0_ffn_gamma, np.float32))
    bet = np.ascontiguousarray(np.asarray(L0_ffn_beta, np.float32))

    if _NC_CACHE is None:
        _NC_CACHE = build_program()
    nc = _NC_CACHE

    in_maps = []
    for core in range(8):
        shard = np.ascontiguousarray(x[2 * core: 2 * core + 2])
        in_maps.append(dict(x=shard, wqkp=wqkp, bqkp=bqkp, wv=wv, bv=bv,
                            wrt=wrt, br=br, w1t=w1t, b1p=b1p, w2t=w2t, b2=b2,
                            gam=gam, bet=bet))
    res = run_bass_kernel_spmd(nc, in_maps, core_ids=list(range(8)))
    out = np.concatenate([np.asarray(r["out"]).reshape(SEQS, S, D)
                          for r in res.results], axis=0)
    return out.astype(np.float32)


# version 09
# speedup vs baseline: 1.2061x; 1.2061x over previous
import contextlib
import numpy as np

import concourse.bass as bass
import concourse.mybir as mybir
from concourse.bass_utils import run_bass_kernel_spmd

F32 = mybir.dt.float32
BF16 = mybir.dt.bfloat16
I32 = mybir.dt.int32
U8 = mybir.dt.uint8
AF = mybir.ActivationFunctionType
ALU = mybir.AluOpType
AX = mybir.AxisListType

SEQS = 2
S = 2048
D = 1024
DK = 8
H = 16
DH = 64
E = 8
FF = 4096
NCH = 16
NT4 = 4
NQT = 8
VSTRIDE = 65
SHIFT = 16.0

ENGS = ("sync", "tensor", "scalar", "vector", "gpsimd")


class Ctr:
    def __init__(self, handle):
        self.h = handle
        self.n = 0

    def bump(self, k):
        self.n += k
        return self.n


class Prog:

    def __init__(self):
        self.ops = {e: [] for e in ENGS}

    def op(self, eng, fn, waits=(), inc=None, inc_k=1):
        w = [(c.h, v) for (c, v) in waits if v > 0]
        val = None
        if inc is not None:
            val = inc.bump(inc_k)
        self.ops[eng].append((fn, w, (inc.h, inc_k) if inc is not None else None))
        return val

    def replay(self, blk):
        def mk(eng_name):
            ops = self.ops[eng_name]

            def run(e):
                guaranteed = {}
                for fn, waits, inc in ops:
                    for (h, v) in waits:
                        if guaranteed.get(id(h), -1) >= v:
                            continue
                        e.wait_ge(h, v)
                        guaranteed[id(h)] = v
                    ins = fn(e)
                    if inc is not None:
                        h, k = inc
                        assert ins is not None
                        ins.then_inc(h, k)
            return run

        blk.sync(mk("sync"))
        blk.tensor(mk("tensor"))
        blk.scalar(mk("scalar"))
        blk.vector(mk("vector"))
        blk.gpsimd(mk("gpsimd"))


class Builder:
    def __init__(self, nc, st):
        self.nc = nc
        self.p = Prog()
        self.st = st

        self.x = nc.dram_tensor("x", [SEQS, S, D], F32, kind="ExternalInput").ap()
        self.wqkp = nc.dram_tensor("wqkp", [D, 2048], F32, kind="ExternalInput").ap()
        self.bqkp = nc.dram_tensor("bqkp", [2048], F32, kind="ExternalInput").ap()
        self.wv = nc.dram_tensor("wv", [D, D], F32, kind="ExternalInput").ap()
        self.bv = nc.dram_tensor("bv", [D], F32, kind="ExternalInput").ap()
        self.wrt = nc.dram_tensor("wrt", [D, E], F32, kind="ExternalInput").ap()
        self.br = nc.dram_tensor("br", [E], F32, kind="ExternalInput").ap()
        self.w1t = nc.dram_tensor("w1t", [E, D, FF], F32, kind="ExternalInput").ap()
        self.b1p = nc.dram_tensor("b1p", [128, E * 32], F32, kind="ExternalInput").ap()
        self.w2t = nc.dram_tensor("w2t", [E, FF, D], F32, kind="ExternalInput").ap()
        self.b2 = nc.dram_tensor("b2", [E, D], F32, kind="ExternalInput").ap()
        self.gam = nc.dram_tensor("gam", [D], F32, kind="ExternalInput").ap()
        self.bet = nc.dram_tensor("bet", [D], F32, kind="ExternalInput").ap()
        self.out = nc.dram_tensor("out", [SEQS, S, D], F32, kind="ExternalOutput").ap()
        self.x1s = nc.dram_tensor("x1s", [SEQS, S, D], F32, kind="Internal").ap()
        self.qscr = nc.dram_tensor("qscr", [SEQS, NQT, 128, 8, 256], BF16,
                                   kind="Internal").ap()

        sb = lambda name, shape, dt: st.enter_context(nc.sbuf_tensor(name, shape, dt))
        self.identf = sb("identf", [128, 128], F32)
        self.idi = sb("idi", [128, 128], I32)
        self.ones_bf = sb("ones_bf", [1, 512], BF16)
        self.ones_f = sb("ones_f", [1, 128], F32)
        self.bqk_bf = sb("bqk_bf", [1, 2048], BF16)
        self.bv_bf = sb("bv_bf", [1, D], BF16)
        self.b1_sb = sb("b1_sb", [128, E * 32], F32)
        self.b2_sb = sb("b2_sb", [E, D], F32)
        self.wrt_sb = sb("wrt_sb", [128, DK * E], F32)
        self.brb = sb("brb", [128, E], F32)
        self.gam_b = sb("gam_b", [128, D], F32)
        self.bet_b = sb("bet_b", [128, D], F32)
        self.router_bf = sb("router_bf", [E, S], BF16)
        self.b2_bf = sb("b2_bf", [E, D], BF16)
        self.er_sel_bf = sb("er_sel_bf", [E, E * 128], BF16)
        self.rbc = sb("rbc", [128, 2 * S], BF16)
        self.consts = sb("consts", [128, 8], F32)
        self.smalls = sb("smalls", [128, 32], F32)
        self.rout = sb("rout", [128, 40], F32)

        self.ARENA = 168448
        self.arena = st.enter_context(nc.sbuf_tensor("arena", [128, self.ARENA], U8))

        ps = lambda name: st.enter_context(nc.psum_tensor(name, [128, 512], F32))
        self.pA, self.pB = ps("pA"), ps("pB")
        self.pC, self.pD = ps("pC"), ps("pD")
        self.pE, self.pF = ps("pE"), ps("pF")

        sem = lambda name: st.enter_context(nc.semaphore(name))
        self.st_ = Ctr(sem("s_t"))
        self.sc_ = Ctr(sem("s_c"))
        self.sv_ = Ctr(sem("s_v"))
        self.sg_ = Ctr(sem("s_g"))
        self.qs_ = Ctr(sem("q_s"))
        self.qx_ = Ctr(sem("q_x"))
        self.qd_ = Ctr(sem("q_d"))
        self.qq_ = Ctr(sem("q_q"))
        self.qxd_ = Ctr(sem("q_xd"))
        self.qo_ = Ctr(sem("q_o"))
        self.qg_ = Ctr(sem("q_g"))
        self.qk0_ = Ctr(sem("q_k0"))
        self.qk1_ = Ctr(sem("q_k1"))
        self.wv0_ = Ctr(sem("q_v0"))
        self.wv1_ = Ctr(sem("q_v1"))
        self.qf_ = Ctr(sem("q_f"))
        self.qt0_ = Ctr(sem("q_t0"))
        self.qt1_ = Ctr(sem("q_t1"))

        self.o_kfm = 0
        self.o_vaug = 32768
        self.o_acc = 66048
        SCR = 131584
        self.SCR = SCR
        self.o_xt = 66048
        self.o_sq_p = 82432
        self.o_ln1bf = 86528
        self.o_win = 90624
        self.o_qwin = 107008
        self.o_wqst = 115200
        self.o_wqbf = 123392
        self.o_wvst = SCR
        self.o_wvbf = SCR + 16384
        self.o_pt = SCR
        self.o_sq_a = SCR + 2048
        self.o_attn = SCR + 8192
        self.o_xch = SCR + 16384
        self.o_tgts = SCR + 20480
        self.o_qqt = SCR + 28672
        self.o_w1b = 0
        self.o_w2b = 32768
        self.o_hblk = 49152
        self.o_wst = 57344
        self.o_tgtfm = SCR
        self.o_hrelu = SCR + 32768
        self.o_otm = 0
        self.o_xo = 4096
        self.o_tmp = 8192
        self.o_sq_o = 12288

    def carve(self, off, n, dt, parts=128):
        nb = n * mybir.dt.size(dt)
        assert off + nb <= self.ARENA, (off, nb)
        return self.arena[0:parts, off:off + nb].bitcast(dt)

    def mm(self, bank, lhsT, rhs, start, stop, waits=(), inc=None):
        return self.p.op(
            "tensor",
            lambda e, b=bank, l=lhsT, r=rhs, s0=start, s1=stop: e.matmul(
                b, l, r, start=s0, stop=s1),
            waits=waits, inc=inc)

    def setup(self):
        p = self.p
        v, sc, g, q = "vector", "scalar", "gpsimd", "sync"
        stage1 = self.carve(0, 2048, F32, parts=1)
        p.op(v, lambda e: e.memset(self.ones_bf[:], 1.0))
        p.op(v, lambda e: e.memset(self.ones_f[:], 1.0))
        p.op(v, lambda e: e.memset(self.consts[:, 0:1], -SHIFT))
        p.op(v, lambda e: e.memset(self.consts[:, 1:2], 1e-5), inc=self.sv_)
        ii = p.op(g, lambda e: e.iota(self.idi[:], [[1, 128]], base=0,
                                      channel_multiplier=-1), inc=self.sg_)
        i2 = p.op(v, lambda e: e.tensor_scalar(self.identf[:], self.idi[:], 0, None,
                                               op0=ALU.is_equal),
                  waits=[(self.sg_, ii)], inc=self.sv_)
        p.op(v, lambda e: e.tensor_copy(
            self.er_sel_bf[:].rearrange("p (j w) -> p j w", w=128),
            self.identf[0:E, 0:E].unsqueeze(2).broadcast_to((E, E, 128))),
            waits=[(self.sv_, i2)], inc=self.sv_)
        p.op(q, lambda e: e.dma_start(
            self.wrt_sb[:].rearrange("p (dk e) -> p dk e", e=E),
            self.wrt.rearrange("(dk p) e -> p dk e", p=128)), inc=self.qs_, inc_k=16)
        p.op(q, lambda e: e.dma_start(self.b1_sb[:], self.b1p), inc=self.qs_, inc_k=16)
        qb2 = p.op(q, lambda e: e.dma_start(self.b2_sb[:], self.b2),
                   inc=self.qs_, inc_k=16)
        p.op(v, lambda e: e.tensor_copy(self.b2_bf[:], self.b2_sb[:]),
             waits=[(self.qs_, qb2)], inc=self.sv_)
        q2 = p.op(q, lambda e: e.dma_start(
            stage1, self.bqkp.rearrange("(one n) -> one n", one=1)),
            inc=self.qs_, inc_k=16)
        c1 = p.op(v, lambda e: e.tensor_copy(self.bqk_bf[:], stage1),
                  waits=[(self.qs_, q2)], inc=self.sv_)
        q3 = p.op(q, lambda e: e.dma_start(
            stage1[0:1, 0:D], self.bv.rearrange("(one n) -> one n", one=1)),
            waits=[(self.sv_, c1)], inc=self.qs_, inc_k=16)
        c2 = p.op(v, lambda e: e.tensor_copy(self.bv_bf[:], stage1[0:1, 0:D]),
                  waits=[(self.qs_, q3)], inc=self.sv_)
        q4 = p.op(q, lambda e: e.dma_start(
            stage1[0:1, 0:D], self.gam.rearrange("(one n) -> one n", one=1)),
            waits=[(self.sv_, c2)], inc=self.qs_, inc_k=16)
        pE_busy = 0
        for half in range(2):
            t1 = self.mm(self.pE[:, 0:512], self.ones_f[0:1, 0:128],
                         stage1[0:1, half * 512: half * 512 + 512],
                         True, True, waits=[(self.qs_, q4),
                                            (self.sc_, pE_busy)], inc=self.st_)
            pE_busy = p.op(sc, lambda e, h=half: e.copy(
                self.gam_b[:, h * 512: h * 512 + 512], self.pE[:, 0:512]),
                waits=[(self.st_, t1)], inc=self.sc_)
        q5 = p.op(q, lambda e: e.dma_start(
            stage1[0:1, 0:D], self.bet.rearrange("(one n) -> one n", one=1)),
            waits=[(self.sc_, pE_busy)], inc=self.qs_, inc_k=16)
        for half in range(2):
            t1 = self.mm(self.pE[:, 0:512], self.ones_f[0:1, 0:128],
                         stage1[0:1, half * 512: half * 512 + 512],
                         True, True, waits=[(self.qs_, q5),
                                            (self.sc_, pE_busy)], inc=self.st_)
            pE_busy = p.op(sc, lambda e, h=half: e.copy(
                self.bet_b[:, h * 512: h * 512 + 512], self.pE[:, 0:512]),
                waits=[(self.st_, t1)], inc=self.sc_)
        q6 = p.op(q, lambda e: e.dma_start(
            stage1[0:1, 0:E], self.br.rearrange("(one n) -> one n", one=1)),
            waits=[(self.sc_, pE_busy)], inc=self.qs_, inc_k=16)
        t1 = self.mm(self.pE[:, 0:8], self.ones_f[0:1, 0:128],
                     stage1[0:1, 0:E], True, True,
                     waits=[(self.qs_, q6), (self.sc_, pE_busy)], inc=self.st_)
        p.op(sc, lambda e: e.copy(self.brb[:], self.pE[:, 0:8]),
             waits=[(self.st_, t1)], inc=self.sc_)

    def ln_stats(self, src_ap_fn, width, eng_waits, sq_off):
        p = self.p
        s1 = self.smalls[:, 0:1]
        s2 = self.smalls[:, 1:2]
        m = self.smalls[:, 2:3]
        msq = self.smalls[:, 3:4]
        var = self.smalls[:, 4:5]
        lnv = self.smalls[:, 5:6]
        rstd = self.smalls[:, 6:7]
        sq = self.carve(sq_off, 1024, F32)
        entry = [(self.sv_, self.sv_.n)]
        a1 = p.op("vector", lambda e: e.reduce_sum(s1, src_ap_fn(), axis=AX.X),
                  waits=list(eng_waits) + entry, inc=self.sv_)
        asq = p.op("vector", lambda e: e.tensor_tensor(
            sq[:, 0:width], src_ap_fn(), src_ap_fn(), op=ALU.mult),
            waits=list(eng_waits), inc=self.sv_)
        p.op("vector", lambda e: e.reduce_sum(s2, sq[:, 0:width], axis=AX.X),
             waits=[(self.sv_, asq)], inc=self.sv_)
        a2 = self.sv_.n
        a3 = p.op("vector", lambda e: e.tensor_scalar(
            m, s1, 1.0 / width, None, op0=ALU.mult),
            waits=[(self.sv_, a1)], inc=self.sv_)
        a4 = p.op("vector", lambda e: e.tensor_tensor(msq, m, m, op=ALU.mult),
                  waits=[(self.sv_, a3)], inc=self.sv_)
        a5 = p.op("vector", lambda e, w=width: e.scalar_tensor_tensor(
            var, s2, 1.0 / w, msq, op0=ALU.mult, op1=ALU.subtract),
            waits=[(self.sv_, a4), (self.sv_, a2)], inc=self.sv_)
        b1 = p.op("scalar", lambda e: e.activation(
            lnv, var, AF.Ln, bias=self.consts[:, 1:2], scale=1.0),
            waits=[(self.sv_, a5)], inc=self.sc_)
        b2 = p.op("scalar", lambda e: e.activation(rstd, lnv, AF.Exp, scale=-0.5),
                  waits=[(self.sc_, b1)], inc=self.sc_)
        return m, rstd, b2

    def stage_p(self, s):
        p = self.p
        kfm = self.carve(self.o_kfm, 8 * 2048, BF16)
        vaug = self.carve(self.o_vaug, 16 * 1040, BF16)
        von = p.op("vector", lambda e: e.memset(
            vaug.rearrange("p (c h z) -> p c h z", h=16, z=VSTRIDE)[:, :, :, 64:65],
            1.0), inc=self.sv_)
        self.vaug_ones = von

        lastln = 0
        win_reads = [0, 0]
        ln1_reads = [0, 0]
        qw_dump = 0
        wq_cast = {}
        last_w_mm = {}
        bankAB_evac = [self.sc_.n, self.sc_.n]
        bankCD_evac = [self.sc_.n, self.sc_.n]
        self.kfm_done_sc = 0
        self.vaug_done_sc = 0
        self.qdump_qs = 0

        for t4 in range(NT4):
            par = t4 % 2
            xt = self.carve(self.o_xt, 4 * 1024, F32)
            qx = p.op("sync", lambda e, s=s, t4=t4, xt=xt: e.dma_start(
                xt.rearrange("p (c d) -> p c d", d=D),
                self.x[s, t4 * 512:(t4 + 1) * 512, :].rearrange(
                    "(c p) d -> p c d", p=128)),
                waits=[(self.sv_, lastln)], inc=self.qx_, inc_k=16)
            tr_done = 0
            for c in range(4):
                cpar = c % 2
                xt_c = lambda xt=xt, c=c: xt[:, c * 1024:(c + 1) * 1024]
                m, rstd, bdone = self.ln_stats(xt_c, D, [(self.qx_, qx)],
                                               self.o_sq_p)
                ln1bf = self.carve(self.o_ln1bf + cpar * 2048, 1024, BF16)
                a6 = p.op("vector", lambda e, xt_c=xt_c, m=m, rstd=rstd,
                          ln1bf=ln1bf: e.scalar_tensor_tensor(
                              ln1bf, xt_c(), m, rstd.broadcast_to((128, 1024)),
                              op0=ALU.subtract, op1=ALU.mult),
                          waits=[(self.sc_, bdone),
                                 ((self.qt0_, self.qt1_)[cpar], ln1_reads[cpar])],
                          inc=self.sv_)
                lastln = a6
                wdst = self.carve(self.o_win + par * 8192, 8 * 512, BF16)
                qtc = (self.qt0_, self.qt1_)[cpar]
                for dk in range(DK):
                    p.op("scalar", lambda e, wdst=wdst, dk=dk, c=c,
                         ln1bf=ln1bf: e.dma_start_transpose(
                             wdst[:, dk * 512 + c * 128: dk * 512 + c * 128 + 128],
                             ln1bf[:, dk * 128:(dk + 1) * 128]),
                         waits=[(self.sv_, a6), (self.st_, win_reads[par])],
                         inc=qtc, inc_k=16)
                ln1_reads[cpar] = qtc.n
            tr_w = [(self.qt0_, self.qt0_.n), (self.qt1_, self.qt1_.n)]
            win = self.carve(self.o_win + par * 8192, 8 * 512, BF16)
            qwin = self.carve(self.o_qwin, 8 * 512, BF16)
            for oc in range(16):
                po = oc % 2
                wqst = self.carve(self.o_wqst + po * 4096, 8 * 128, F32)
                wqbf = self.carve(self.o_wqbf + po * 2048, 8 * 128, BF16)
                qsem = (self.qk0_, self.qk1_)[po]
                qd = p.op("sync", lambda e, oc=oc, wqst=wqst: e.dma_start(
                    wqst.rearrange("p (dk o) -> p dk o", o=128),
                    self.wqkp[:, oc * 128:(oc + 1) * 128].rearrange(
                        "(dk p) o -> p dk o", p=128)),
                    waits=[(self.sv_, wq_cast.get(("qk", po), 0))],
                    inc=qsem, inc_k=16)
                cc = p.op("vector", lambda e, wqbf=wqbf, wqst=wqst: e.tensor_copy(
                    wqbf, wqst),
                    waits=[(qsem, qd), (self.st_, last_w_mm.get(("qk", po), 0))],
                    inc=self.sv_)
                wq_cast[("qk", po)] = cc
                bank = (self.pA, self.pB)[oc % 2]
                for dk in range(DK):
                    self.mm(bank[:, 0:512],
                            wqbf[:, dk * 128:(dk + 1) * 128],
                            win[:, dk * 512:(dk + 1) * 512],
                            start=(dk == 0), stop=False,
                            waits=(([(self.sv_, cc),
                                     (self.sc_, bankAB_evac[oc % 2])] + tr_w)
                                   if dk == 0 else ()))
                tmm = self.mm(bank[:, 0:512],
                              self.bqk_bf[0:1, oc * 128:(oc + 1) * 128],
                              self.ones_bf[0:1, 0:512], start=False, stop=True,
                              inc=self.st_)
                last_w_mm[("qk", po)] = tmm
                if oc < 8:
                    dst = qwin.rearrange("p (j o t) -> p j o t",
                                         j=2, o=8)[:, :, oc, :]
                    w_extra = [(self.qd_, qw_dump)]
                    ev = p.op("scalar", lambda e, dst=dst, bank=bank:
                              e.activation(
                                  dst,
                                  bank[:, 0:512].rearrange("p (j t) -> p j t",
                                                           t=256),
                                  AF.Copy, scale=1.0 / 32.0),
                              waits=[(self.st_, tmm)] + w_extra, inc=self.sc_)
                else:
                    dst = kfm[:, (oc - 8) * 2048 + t4 * 512:
                              (oc - 8) * 2048 + t4 * 512 + 512]
                    ev = p.op("scalar", lambda e, dst=dst, bank=bank:
                              e.activation(
                                  dst, bank[:, 0:512], AF.Copy,
                                  scale=1.0 / 32.0),
                              waits=[(self.st_, tmm)], inc=self.sc_)
                bankAB_evac[oc % 2] = ev
                if oc == 15:
                    self.kfm_done_sc = ev
            qev = self.sc_.n
            for j in range(2):
                qw_dump = p.op("sync", lambda e, s=s, t4=t4, j=j, qwin=qwin:
                               e.dma_start(
                                   self.qscr[s, 2 * t4 + j],
                                   qwin[:, j * 2048:(j + 1) * 2048].rearrange(
                                       "p (o t) -> p o t", t=256)),
                               waits=[(self.sc_, qev)], inc=self.qd_, inc_k=16)
            self.qdump_qs = qw_dump
            for vc in range(4):
                pv = vc % 2
                wvst = self.carve(self.o_wvst + pv * 8192, 8 * 256, F32)
                wvbf = self.carve(self.o_wvbf + pv * 4096, 8 * 256, BF16)
                vsem = (self.wv0_, self.wv1_)[pv]
                qd = p.op("sync", lambda e, vc=vc, wvst=wvst: e.dma_start(
                    wvst.rearrange("p (dk o) -> p dk o", o=256),
                    self.wv[:, vc * 256:(vc + 1) * 256].rearrange(
                        "(dk p) o -> p dk o", p=128)),
                    waits=[(self.sv_, wq_cast.get(("v", pv), 0))],
                    inc=vsem, inc_k=16)
                cc = p.op("vector", lambda e, wvbf=wvbf, wvst=wvst: e.tensor_copy(
                    wvbf, wvst),
                    waits=[(vsem, qd), (self.st_, last_w_mm.get(("v", pv), 0))],
                    inc=self.sv_)
                wq_cast[("v", pv)] = cc
                for c in range(4):
                    bpar = (vc * 4 + c) % 2
                    bank = (self.pC, self.pD)[bpar]
                    for dk in range(DK):
                        self.mm(bank[:, 0:256],
                                win[:, dk * 512 + c * 128: dk * 512 + c * 128 + 128],
                                wvbf[:, dk * 256:(dk + 1) * 256],
                                start=(dk == 0), stop=False,
                                waits=(([(self.sv_, cc),
                                         (self.sc_, bankCD_evac[bpar])] + tr_w)
                                       if dk == 0 else ()))
                    tmm = self.mm(bank[:, 0:256], self.ones_bf[0:1, 0:128],
                                  self.bv_bf[0:1, vc * 256:(vc + 1) * 256],
                                  start=False, stop=True, inc=self.st_)
                    last_w_mm[("v", pv)] = tmm
                    gch = t4 * 4 + c
                    dst = vaug[:, gch * 1040: (gch + 1) * 1040].rearrange(
                        "p (h z) -> p h z", z=VSTRIDE)[:, vc * 4:(vc + 1) * 4, 0:64]
                    ev = p.op("scalar", lambda e, dst=dst, bank=bank: e.activation(
                        dst, bank[:, 0:256].rearrange("p (h w) -> p h w", w=64),
                        AF.Copy, scale=1.0 / 32.0),
                        waits=[(self.st_, tmm), (self.sv_, von)], inc=self.sc_)
                    bankCD_evac[bpar] = ev
            win_reads[par] = self.st_.n
            self.vaug_done_sc = self.sc_.n
        self.stageP_end = dict(sv=self.sv_.n, sc=self.sc_.n, st=self.st_.n,
                               sg=self.sg_.n)

    def stage_a(self, s):
        p = self.p
        kfm = self.carve(self.o_kfm, 8 * 2048, BF16)
        vaug = self.carve(self.o_vaug, 16 * 1040, BF16)
        acc = self.carve(self.o_acc, 8 * 2048, F32)
        pt = self.carve(self.o_pt, 2 * 512, BF16)
        attn = self.carve(self.o_attn, 2 * 1024, F32)
        xch = self.carve(self.o_xch, 1024, F32)

        pt_war = [0, 0]
        entryA = [(self.st_, self.stageP_end["st"]),
                  (self.sg_, self.stageP_end["sg"]),
                  (self.sc_, self.stageP_end["sc"]),
                  (self.sv_, self.stageP_end["sv"])]
        qreload_war = 0
        qchain = 0
        bankAB_evac = [self.stageP_end["sc"], self.stageP_end["sc"]]
        bankCD_free = [self.stageP_end["sc"], self.stageP_end["sc"]]
        attn_war = [0, 0]
        xch_war = 0
        xch_war_qs = 0
        tgts_war = [0, 0]
        pv_stop = [0, 0]
        pending_back = None
        pEF_free = [0, 0]
        self.acc_sc = 0
        self.router_sc = 0
        rs0 = self.smalls[:, 8:9]
        rs1 = self.smalls[:, 9:10]

        for qt in range(NQT):
            qq = qt % 2
            qqt = self.carve(self.o_qqt + qq * 4096, 8 * 256, BF16)
            qr = p.op("sync", lambda e, s=s, qt=qt, qqt=qqt: e.dma_start(
                qqt.rearrange("p (oc t) -> p oc t", t=256),
                self.qscr[s, qt]),
                waits=[(self.qd_, self.qdump_qs), (self.st_, qreload_war),
                       (self.st_, qchain)],
                inc=self.qq_, inc_k=16)
            qt_scores_start = self.st_.n
            for h in range(H):
                hp = (h % 2) * 64
                hc = h // 2
                evs = {}

                def emit_pv(kp, h=h):
                    nonlocal pv_stop
                    ptb = pt[:, (kp % 2) * 512:(kp % 2) * 512 + 512]
                    ev = evs[kp]
                    for half in range(2):
                        kc = 2 * kp + half
                        for q2 in range(2):
                            bank = (self.pC, self.pD)[q2]
                            if half == 0 and q2 == 0:
                                w = [(self.sc_, ev)]
                                if kc == 0:
                                    w += [(self.sc_, bankCD_free[0]),
                                          (self.sc_, bankCD_free[1]),
                                          (self.sc_, self.vaug_done_sc),
                                          (self.sv_, self.vaug_ones)]
                            else:
                                w = ()
                            self.mm(
                                bank[:, 0:65],
                                ptb[:, half * 256 + q2 * 128:
                                    half * 256 + q2 * 128 + 128],
                                vaug[:, kc * 1040 + h * VSTRIDE:
                                     kc * 1040 + h * VSTRIDE + VSTRIDE],
                                start=(kc == 0), stop=(kc == NCH - 1),
                                waits=w,
                                inc=(self.st_ if kc == NCH - 1 else None))
                    pt_war[kp % 2] = self.st_.n
                    if kp == 7:
                        pv_stop = [self.st_.n - 1, self.st_.n]

                for kp in range(8):
                    sbank = (self.pA, self.pB)[kp % 2]
                    ptb = pt[:, (kp % 2) * 512:(kp % 2) * 512 + 512]
                    for half in range(2):
                        kc = 2 * kp + half
                        tmm = self.mm(
                            sbank[:, half * 256: half * 256 + 256],
                            kfm[hp:hp + 64,
                                hc * 2048 + kc * 128: hc * 2048 + kc * 128 + 128],
                            qqt[hp:hp + 64, hc * 256: hc * 256 + 256],
                            start=True, stop=True,
                            waits=([(self.qq_, qr),
                                    (self.sc_, bankAB_evac[kp % 2]),
                                    (self.st_, pt_war[kp % 2]),
                                    (self.sc_, self.kfm_done_sc)]
                                   if half == 0 else ()),
                            inc=(self.st_ if half == 1 else None))
                        if h == 0 and kp == 0 and half == 0:
                            qchain = self.st_.n + 1
                    ev = p.op("scalar", lambda e, sbank=sbank, ptb=ptb:
                              e.activation(
                                  ptb, sbank[:, 0:512],
                                  AF.Exp, bias=self.consts[:, 0:1],
                                  scale=1.0 / 64.0),
                              waits=[(self.st_, tmm)] + entryA, inc=self.sc_)
                    bankAB_evac[kp % 2] = ev
                    evs[kp] = ev
                    if kp >= 1:
                        emit_pv(kp - 1)
                emit_pv(7)
                for q2 in range(2):
                    bank = (self.pC, self.pD)[q2]
                    tpv = pv_stop[q2]
                    rs = (rs0, rs1)[q2]
                    rv = p.op("vector", lambda e, bank=bank, rs=rs: e.reciprocal(
                        rs, bank[:, 64:65]),
                        waits=[(self.st_, tpv)], inc=self.sv_)
                    ev = p.op("scalar", lambda e, bank=bank, rs=rs, h=h, q2=q2,
                              attn=attn: e.activation(
                                  attn[:, q2 * 1024 + h * 64:
                                       q2 * 1024 + h * 64 + 64],
                                  bank[:, 0:64], AF.Copy, scale=rs),
                              waits=[(self.sv_, rv), (self.sv_, attn_war[q2])]
                                    + entryA,
                              inc=self.sc_)
                    bankCD_free[q2] = ev
            qreload_war = qt_scores_start
            attn_done = self.sc_.n
            if pending_back is not None:
                pending_back()
                pending_back = None
            front_state = []
            for q2 in range(2):
                g = qt * 2 + q2
                tgts = self.carve(self.o_tgts + q2 * 4096, 1024, F32)
                attn_c = lambda attn=attn, q2=q2: attn[:, q2 * 1024:(q2 + 1) * 1024]
                m, rstd, bdone = self.ln_stats(attn_c, D, [(self.sc_, attn_done)],
                                               self.o_sq_a)
                qx = p.op("sync", lambda e, s=s, g=g, xch=xch: e.dma_start(
                    xch, self.x[s, g * 128:(g + 1) * 128, :]),
                    waits=[(self.sv_, xch_war), (self.qxd_, xch_war_qs)]
                          + entryA,
                    inc=self.qx_, inc_k=16)
                a1 = p.op("vector", lambda e, attn_c=attn_c, m=m, rstd=rstd,
                          tgts=tgts: e.scalar_tensor_tensor(
                              tgts, attn_c(), m, rstd.broadcast_to((128, D)),
                              op0=ALU.subtract, op1=ALU.mult),
                          waits=[(self.sc_, bdone), (self.st_, tgts_war[q2])]
                                + entryA,
                          inc=self.sv_)
                attn_war[q2] = a1
                a2 = p.op("vector", lambda e, xch=xch, tgts=tgts: e.tensor_tensor(
                    xch, xch, tgts, op=ALU.add),
                    waits=[(self.sv_, a1), (self.qx_, qx)], inc=self.sv_)
                xch_f = lambda xch=xch: xch[:, 0:D]
                m2, rstd2, b2d = self.ln_stats(xch_f, D, [(self.sv_, a2)],
                                               self.o_sq_a)
                a3 = p.op("vector", lambda e, xch=xch, m2=m2, rstd2=rstd2,
                          tgts=tgts: e.scalar_tensor_tensor(
                              tgts, xch, m2, rstd2.broadcast_to((128, D)),
                              op0=ALU.subtract, op1=ALU.mult),
                          waits=[(self.sc_, b2d)], inc=self.sv_)
                qd = p.op("sync", lambda e, s=s, g=g, xch=xch: e.dma_start(
                    self.x1s[s, g * 128:(g + 1) * 128, :], xch),
                    waits=[(self.sv_, a2)], inc=self.qxd_, inc_k=16)
                xch_war = a3
                xch_war_qs = qd
                front_state.append((g, q2, a3, tgts))

            def back(fs=tuple(front_state)):
                for (g, q2, a3, tgts) in fs:
                    for dk in range(DK):
                        bank = (self.pE, self.pF)[dk % 2]
                        t1 = p.op("tensor", lambda e, bank=bank, tgts=tgts,
                                  dk=dk: e.transpose(
                                      bank[:, 0:128],
                                      tgts[:, dk * 128:(dk + 1) * 128],
                                      self.identf[:]),
                                  waits=[(self.sv_, a3),
                                         (self.sc_, pEF_free[dk % 2])],
                                  inc=self.st_)
                        cp = p.op("scalar", lambda e, bank=bank, dk=dk, g=g,
                                  acc=acc: e.copy(
                                      acc[:, dk * 2048 + g * 128:
                                          dk * 2048 + g * 128 + 128],
                                      bank[:, 0:128]),
                                  waits=[(self.st_, t1)], inc=self.sc_)
                        pEF_free[dk % 2] = cp
                    tgts_war[q2] = self.st_.n
                    acc_done = self.sc_.n
                    self.acc_sc = acc_done
                    for dk in range(DK):
                        self.mm(self.pF[:, 0:8],
                                acc[:, dk * 2048 + g * 128:
                                    dk * 2048 + g * 128 + 128],
                                self.wrt_sb[:, dk * 8:(dk + 1) * 8],
                                start=(dk == 0), stop=(dk == DK - 1),
                                waits=([(self.sc_, acc_done)] if dk == 0 else ()),
                                inc=(self.st_ if dk == DK - 1 else None))
                    tlg = self.st_.n
                    self.emit_router_chain(g, tlg)
                    pEF_free[1] = max(pEF_free[1], self.router_pf_free)
            pending_back = back
        if pending_back is not None:
            pending_back()
            pending_back = None
        self.x1dump_qxd = self.qxd_.n
        self.stageA_end = dict(sv=self.sv_.n, sc=self.sc_.n, st=self.st_.n,
                               sg=self.sg_.n)

    def emit_router_chain(self, g, tlg):
        p = self.p
        lg = self.rout[:, 0:8]
        d = self.rout[:, 8:16]
        mask = self.rout[:, 16:24]
        d2 = self.rout[:, 24:32]
        w = self.rout[:, 32:40]
        m1 = self.smalls[:, 16:17]
        m2 = self.smalls[:, 17:18]
        e2 = self.smalls[:, 18:19]
        sS = self.smalls[:, 19:20]
        rs = self.smalls[:, 20:21]
        entry = [(self.sv_, self.sv_.n)]
        a1 = p.op("vector", lambda e: e.tensor_tensor(lg, self.pF[:, 0:8],
                                                      self.brb[:], op=ALU.add),
                  waits=[(self.st_, tlg)] + entry, inc=self.sv_)
        a2 = p.op("vector", lambda e: e.reduce_max(m1, lg, axis=AX.X),
                  waits=[(self.sv_, a1)], inc=self.sv_)
        a3 = p.op("vector", lambda e: e.tensor_scalar(d, lg, m1, None,
                                                      op0=ALU.subtract),
                  waits=[(self.sv_, a2)], inc=self.sv_)
        a4 = p.op("vector", lambda e: e.tensor_scalar(mask, d, 0.0, None,
                                                      op0=ALU.is_equal),
                  waits=[(self.sv_, a3)], inc=self.sv_)
        a5 = p.op("vector", lambda e: e.scalar_tensor_tensor(
            d2, mask, -1e9, d, op0=ALU.mult, op1=ALU.add),
            waits=[(self.sv_, a4)], inc=self.sv_)
        a6 = p.op("vector", lambda e: e.reduce_max(m2, d2, axis=AX.X),
                  waits=[(self.sv_, a5)], inc=self.sv_)
        b1 = p.op("scalar", lambda e: e.activation(e2, m2, AF.Exp, scale=1.0),
                  waits=[(self.sv_, a6)], inc=self.sc_)
        a7 = p.op("vector", lambda e: e.tensor_scalar(sS, e2, 1.0, None,
                                                      op0=ALU.add),
                  waits=[(self.sc_, b1)], inc=self.sv_)
        a8 = p.op("vector", lambda e: e.reciprocal(rs, sS),
                  waits=[(self.sv_, a7)], inc=self.sv_)
        a9 = p.op("vector", lambda e: e.tensor_scalar(mask, d, m2, None,
                                                      op0=ALU.is_ge),
                  waits=[(self.sv_, a6)], inc=self.sv_)
        b2 = p.op("scalar", lambda e: e.activation(lg, d, AF.Exp, scale=1.0),
                  waits=[(self.sv_, a9)], inc=self.sc_)
        a10 = p.op("vector", lambda e: e.scalar_tensor_tensor(
            w, lg, rs, mask, op0=ALU.mult, op1=ALU.mult),
            waits=[(self.sc_, b2), (self.sv_, a8)], inc=self.sv_)
        t1 = p.op("tensor", lambda e: e.transpose(
            self.pF[0:8, 128:256], w, self.identf[:]),
            waits=[(self.sv_, a10)], inc=self.st_)
        c1 = p.op("scalar", lambda e, g=g: e.copy(
            self.router_bf[0:8, g * 128:(g + 1) * 128], self.pF[0:8, 128:256]),
            waits=[(self.st_, t1)], inc=self.sc_)
        self.router_pf_free = c1
        self.router_sc = c1

    def stage_f(self, s):
        p = self.p
        acc = self.carve(self.o_acc, 8 * 2048, F32)
        tgtfm = self.carve(self.o_tgtfm, 8 * 2048, BF16)
        hblk = self.carve(self.o_hblk, 8 * 512, BF16)
        A_end = self.stageA_end
        kv_free = [(self.st_, A_end["st"]), (self.sc_, A_end["sc"]),
                   (self.sv_, A_end["sv"]), (self.qxd_, self.x1dump_qxd),
                   (self.qx_, self.qx_.n), (self.qq_, self.qq_.n)]

        cast_done = 0
        for dk in range(DK):
            cast_done = p.op("vector", lambda e, dk=dk, acc=acc, tgtfm=tgtfm:
                             e.tensor_copy(tgtfm[:, dk * 2048:(dk + 1) * 2048],
                                           acc[:, dk * 2048:(dk + 1) * 2048]),
                             waits=[(self.sc_, self.acc_sc)] + kv_free,
                             inc=self.sv_)
        self.tgtfm_sv = cast_done

        pEF_free = [A_end["sc"], A_end["sc"]]
        pEF_free_v = [0, 0]
        for tt in range(4):
            for dk in range(DK):
                bp = dk % 2
                bank = (self.pE, self.pF)[bp]
                tmm = self.mm(bank[:, 0:512],
                              self.b2_bf[0:8, dk * 128:(dk + 1) * 128],
                              self.router_bf[0:8, tt * 512:(tt + 1) * 512],
                              start=True, stop=True,
                              waits=[(self.sc_, self.router_sc),
                                     (self.sv_, pEF_free_v[bp]),
                                     (self.sv_, cast_done)],
                              inc=self.st_)
                av = p.op("vector", lambda e, bank=bank, dk=dk, tt=tt, acc=acc:
                          e.tensor_tensor(
                              acc[:, dk * 2048 + tt * 512:
                                  dk * 2048 + tt * 512 + 512],
                              bank[:, 0:512],
                              acc[:, dk * 2048 + tt * 512:
                                  dk * 2048 + tt * 512 + 512],
                              op=ALU.add),
                          waits=[(self.st_, tmm)], inc=self.sv_)
                pEF_free_v[bp] = av

        lane_buf = [self.carve(self.o_wst + l * 2048, 512, F32)
                    for l in range(4)]
        lane_q = [self.qf_, self.qg_, self.qf_, self.qg_]
        lane_eng = ["sync", "gpsimd", "sync", "gpsimd"]
        lane_cast = [None, None, None, None]

        st_box = dict(w1_mm_war=[A_end["st"], A_end["st"]],
                      w2_mm_war=A_end["st"])
        rbc_sc = [0, 0]
        rbc_war_sv = [0, 0]
        bankAB_evac = [A_end["sc"], A_end["sc"]]
        bankCD_free = [max(pEF_free_v), max(pEF_free_v)]
        hrelu_war = [0, 0]
        hblk_war = [A_end["st"]]
        pE_free_sc = [A_end["sc"]]
        pE_free_v = max(pEF_free_v)
        h_last_sv = [0]

        blocks = [(e_, fb) for e_ in range(E) for fb in range(4)]
        NB = len(blocks)
        w1c = [[0, 0] for _ in range(NB)]
        w2c = [[0, 0] for _ in range(NB)]

        w1_qd = {}
        w2_qd = {}

        def _load(qd_map, i, k, src_fn):
            l = k % 4
            w = [lane_cast[l]] if lane_cast[l] is not None else []
            qd_map[(i, k)] = p.op(
                lane_eng[l], lambda e, src_fn=src_fn, l=l:
                e.dma_start(lane_buf[l], src_fn()),
                waits=w + kv_free, inc=lane_q[l], inc_k=16)

        def _cast(qd_map, i, k, dst, war, cmax):
            l = k % 4
            ve = (k % 2 == 0)
            ctr = self.sv_ if ve else self.sc_
            fn = ((lambda e, dst=dst, l=l: e.tensor_copy(dst, lane_buf[l]))
                  if ve else
                  (lambda e, dst=dst, l=l: e.copy(dst, lane_buf[l])))
            cc = p.op("vector" if ve else "scalar", fn,
                      waits=[(lane_q[l], qd_map[(i, k)]), (self.st_, war)],
                      inc=ctr)
            lane_cast[l] = (ctr, cc)
            idx = 0 if ve else 1
            cmax[i][idx] = max(cmax[i][idx], cc)

        def w1_load(i, k):
            e_, fb = blocks[i]
            dk, c = k // 2, k % 2
            _load(w1_qd, i, k, lambda e_=e_, fb=fb, dk=dk, c=c:
                  self.w1t[e_, dk * 128:(dk + 1) * 128,
                           fb * 1024 + c * 512: fb * 1024 + (c + 1) * 512])

        def w1_cast(i, k):
            dk, c = k // 2, k % 2
            w1b = self.carve(self.o_w1b + (i % 2) * 16384, 8 * 1024, BF16)
            _cast(w1_qd, i, k,
                  w1b[:, dk * 1024 + c * 512: dk * 1024 + (c + 1) * 512],
                  st_box["w1_mm_war"][i % 2], w1c)

        def w2_load(i, k):
            e_, fb = blocks[i]
            fc, c = k // 2, k % 2
            _load(w2_qd, i, k, lambda e_=e_, fb=fb, fc=fc, c=c:
                  self.w2t[e_, fb * 1024 + fc * 128: fb * 1024 + (fc + 1) * 128,
                           c * 512:(c + 1) * 512])

        def w2_cast(i, k):
            fc, c = k // 2, k % 2
            w2b = self.carve(self.o_w2b, 8 * 1024, BF16)
            _cast(w2_qd, i, k,
                  w2b[:, fc * 1024 + c * 512: fc * 1024 + (c + 1) * 512],
                  st_box["w2_mm_war"], w2c)

        def emit_rbc(e_):
            ep = e_ % 2
            for tt in range(4):
                tmm = self.mm(self.pE[:, 0:512],
                              self.er_sel_bf[0:E, e_ * 128:(e_ + 1) * 128],
                              self.router_bf[0:E, tt * 512:(tt + 1) * 512],
                              start=True, stop=True,
                              waits=[(self.sc_, self.router_sc),
                                     (self.sv_, pE_free_v),
                                     (self.sc_, pE_free_sc[0]),
                                     (self.sv_, rbc_war_sv[ep])],
                              inc=self.st_)
                cv = p.op("scalar", lambda e, tt=tt, ep=ep: e.copy(
                    self.rbc[:, ep * 2048 + tt * 512: ep * 2048 + tt * 512 + 512],
                    self.pE[:, 0:512]),
                    waits=[(self.st_, tmm)], inc=self.sc_)
                pE_free_sc[0] = cv
            rbc_sc[ep] = pE_free_sc[0]

        def emit_compute(i, slot_ops):
            e_, fb = blocks[i]
            ep = e_ % 2
            wpar = i % 2
            w1b = self.carve(self.o_w1b + wpar * 16384, 8 * 1024, BF16)
            w2b = self.carve(self.o_w2b, 8 * 1024, BF16)
            slot = 0
            for tt in range(4):
                for fc in range(8):
                    for f in slot_ops.get(slot, ()):
                        f()
                    slot += 1
                    bp = fc % 2
                    bank = (self.pA, self.pB)[bp]
                    for dk in range(DK):
                        self.mm(bank[:, 0:512],
                                w1b[:, dk * 1024 + fc * 128:
                                    dk * 1024 + fc * 128 + 128],
                                tgtfm[:, dk * 2048 + tt * 512:
                                      dk * 2048 + tt * 512 + 512],
                                start=(dk == 0), stop=(dk == DK - 1),
                                waits=([(self.sv_, w1c[i][0]),
                                        (self.sc_, w1c[i][1]),
                                        (self.sv_, self.tgtfm_sv),
                                        (self.sc_, bankAB_evac[bp])]
                                       if dk == 0 else ()),
                                inc=(self.st_ if dk == DK - 1 else None))
                    tmm = self.st_.n
                    hre = self.carve(self.o_hrelu + (fc % 2) * 1024, 512, BF16)
                    ev = p.op("scalar", lambda e, bank=bank, e_=e_, fb=fb,
                              fc=fc, hre=hre: e.activation(
                                  hre, bank[:, 0:512], AF.Relu,
                                  bias=self.b1_sb[:, e_ * 32 + fb * 8 + fc:
                                                  e_ * 32 + fb * 8 + fc + 1],
                                  scale=1.0),
                              waits=[(self.st_, tmm),
                                     (self.sv_, hrelu_war[fc % 2])]
                                    + kv_free,
                              inc=self.sc_)
                    bankAB_evac[bp] = ev
                    hm = p.op("vector", lambda e, hre=hre, fc=fc, tt=tt,
                              ep=ep, hblk=hblk: e.tensor_tensor(
                                  hblk[:, fc * 512:(fc + 1) * 512], hre,
                                  self.rbc[:, ep * 2048 + tt * 512:
                                           ep * 2048 + tt * 512 + 512],
                                  op=ALU.mult),
                              waits=[(self.sc_, ev), (self.sc_, rbc_sc[ep]),
                                     (self.st_, hblk_war[0])],
                              inc=self.sv_)
                    hrelu_war[fc % 2] = hm
                    h_last_sv[0] = hm
                st_box["w1_mm_war"][wpar] = self.st_.n
                for dk in range(DK):
                    bp = dk % 2
                    bank = (self.pC, self.pD)[bp]
                    for fc in range(8):
                        self.mm(bank[:, 0:512],
                                w2b[:, fc * 1024 + dk * 128:
                                    fc * 1024 + dk * 128 + 128],
                                hblk[:, fc * 512:(fc + 1) * 512],
                                start=(fc == 0), stop=(fc == 7),
                                waits=([(self.sv_, h_last_sv[0]),
                                        (self.sv_, w2c[i][0]),
                                        (self.sc_, w2c[i][1]),
                                        (self.sv_, bankCD_free[bp])]
                                       if fc == 0 else ()),
                                inc=(self.st_ if fc == 7 else None))
                    tmm = self.st_.n
                    av = p.op("vector", lambda e, bank=bank, dk=dk, tt=tt,
                              acc=acc: e.tensor_tensor(
                                  acc[:, dk * 2048 + tt * 512:
                                      dk * 2048 + tt * 512 + 512],
                                  bank[:, 0:512],
                                  acc[:, dk * 2048 + tt * 512:
                                      dk * 2048 + tt * 512 + 512],
                                  op=ALU.add),
                              waits=[(self.st_, tmm)], inc=self.sv_)
                    bankCD_free[bp] = av
                st_box["w2_mm_war"] = self.st_.n
                hblk_war[0] = self.st_.n
            if fb == 3:
                rbc_war_sv[ep] = h_last_sv[0]

        for k in range(16):
            if k >= 4:
                w1_cast(0, k - 4)
            w1_load(0, k)
        for k in range(12, 16):
            w1_cast(0, k)
        for k in range(16):
            if k >= 4:
                w2_cast(0, k - 4)
            w2_load(0, k)
        for k in range(12, 16):
            w2_cast(0, k)
        for i in range(NB):
            e_, fb = blocks[i]
            slot_ops = {}
            if i > 0:
                for j in range(8):
                    slot_ops.setdefault(j, []).append(
                        lambda j=j, i=i: w2_cast(i, 2 * j))
                    slot_ops.setdefault(j, []).append(
                        lambda j=j, i=i: w2_cast(i, 2 * j + 1))
                    if 2 * j + 4 < 16:
                        slot_ops.setdefault(j, []).append(
                            lambda j=j, i=i: w2_load(i, 2 * j + 4))
                        slot_ops.setdefault(j, []).append(
                            lambda j=j, i=i: w2_load(i, 2 * j + 5))
            if i + 1 < NB:
                for k in range(16):
                    if k >= 4:
                        slot_ops.setdefault(8 + k, []).append(
                            lambda k=k, i=i: w1_cast(i + 1, k - 4))
                    slot_ops.setdefault(8 + k, []).append(
                        lambda k=k, i=i: w1_load(i + 1, k))
                for k in range(12, 16):
                    slot_ops.setdefault(16 + k, []).append(
                        lambda k=k, i=i: w1_cast(i + 1, k))
                for c in range(4):
                    slot_ops.setdefault(28 + c, []).append(
                        lambda c=c, i=i: w2_load(i + 1, c))
            if fb == 0:
                emit_rbc(e_)
            emit_compute(i, slot_ops)
        self.stageF_end = dict(sv=self.sv_.n, sc=self.sc_.n, st=self.st_.n,
                               sg=self.sg_.n)

    def stage_o(self, s):
        p = self.p
        acc = self.carve(self.o_acc, 8 * 2048, F32)
        otm = self.carve(self.o_otm, 1024, F32)
        xo = self.carve(self.o_xo, 1024, F32)
        tmp = self.carve(self.o_tmp, 1024, F32)
        F_end = self.stageF_end
        pEF_free = [F_end["sc"], F_end["sc"]]
        acc_ready = [(self.sv_, F_end["sv"]), (self.sg_, F_end["sg"])]
        otm_war = 0
        otm_war_qs = 0
        xo_war = 0
        for g in range(NCH):
            cps = 0
            for dk in range(DK):
                bank = (self.pE, self.pF)[dk % 2]
                t1 = p.op("tensor", lambda e, bank=bank, dk=dk, g=g, acc=acc:
                          e.transpose(bank[:, 0:128],
                                      acc[:, dk * 2048 + g * 128:
                                          dk * 2048 + g * 128 + 128],
                                      self.identf[:]),
                          waits=acc_ready + [(self.sc_, pEF_free[dk % 2]),
                                             (self.sv_, otm_war),
                                             (self.qo_, otm_war_qs)],
                          inc=self.st_)
                cp = p.op("scalar", lambda e, bank=bank, dk=dk, otm=otm:
                          e.copy(otm[:, dk * 128:(dk + 1) * 128], bank[:, 0:128]),
                          waits=[(self.st_, t1)], inc=self.sc_)
                pEF_free[dk % 2] = cp
                cps = cp
            otm_f = lambda otm=otm: otm[:, 0:D]
            m, rstd, bdone = self.ln_stats(otm_f, D, [(self.sc_, cps)],
                                           self.o_sq_o)
            a1 = p.op("vector", lambda e, otm=otm, m=m, rstd=rstd, tmp=tmp:
                      e.scalar_tensor_tensor(
                          tmp, otm, m, rstd.broadcast_to((128, D)),
                          op0=ALU.subtract, op1=ALU.mult),
                      waits=[(self.sc_, bdone)], inc=self.sv_)
            a2 = p.op("vector", lambda e, tmp=tmp: e.tensor_tensor(
                tmp, tmp, self.gam_b[:], op=ALU.mult),
                waits=[(self.sv_, a1)], inc=self.sv_)
            a3 = p.op("vector", lambda e, tmp=tmp: e.tensor_tensor(
                tmp, tmp, self.bet_b[:], op=ALU.add),
                waits=[(self.sv_, a2)], inc=self.sv_)
            tmp_f = lambda tmp=tmp: tmp[:, 0:D]
            m3, rstd3, b3done = self.ln_stats(tmp_f, D, [(self.sv_, a3)],
                                              self.o_sq_o)
            qx = p.op("sync", lambda e, s=s, g=g, xo=xo: e.dma_start(
                xo, self.x1s[s, g * 128:(g + 1) * 128, :]),
                waits=[(self.sv_, xo_war), (self.qxd_, self.x1dump_qxd)],
                inc=self.qx_, inc_k=16)
            a4 = p.op("vector", lambda e, tmp=tmp, m3=m3, rstd3=rstd3, otm=otm:
                      e.scalar_tensor_tensor(
                          otm, tmp, m3, rstd3.broadcast_to((128, D)),
                          op0=ALU.subtract, op1=ALU.mult),
                      waits=[(self.sc_, b3done)], inc=self.sv_)
            a5 = p.op("vector", lambda e, otm=otm, xo=xo: e.tensor_tensor(
                otm, otm, xo, op=ALU.add),
                waits=[(self.sv_, a4), (self.qx_, qx)], inc=self.sv_)
            xo_war = a5
            otm_war_qs = p.op("sync", lambda e, s=s, g=g, otm=otm: e.dma_start(
                self.out[s, g * 128:(g + 1) * 128, :], otm),
                waits=[(self.sv_, a5)], inc=self.qo_, inc_k=16)
            otm_war = a5

    def seq_barrier(self):
        p = self.p
        marks = [(c, c.n) for c in (self.st_, self.sc_, self.sv_, self.sg_,
                 self.qs_, self.qx_, self.qd_, self.qq_, self.qxd_, self.qo_,
                 self.qk0_, self.qk1_, self.wv0_, self.wv1_, self.qf_,
                 self.qg_, self.qt0_, self.qt1_)]
        marks = [(c, v) for (c, v) in marks]
        for eng in ENGS:
            p.op(eng, lambda e: e.nop(), waits=marks)

    def finish(self):
        self.p.op("sync", lambda e: e.nop(), waits=[(self.qo_, self.qo_.n)])

    def build(self, stop_after=None, nseqs=SEQS):
        self.setup()
        self.seq_barrier()
        if stop_after == "S":
            self.finish()
            with self.nc.Block() as blk:
                self.p.replay(blk)
            return
        for s in range(nseqs):
            if s > 0:
                self.seq_barrier()
            self.stage_p(s)
            if stop_after == "P":
                break
            self.stage_a(s)
            if stop_after == "A":
                break
            self.stage_f(s)
            if stop_after == "F":
                break
            self.stage_o(s)
        self.finish()
        with self.nc.Block() as blk:
            self.p.replay(blk)


def build_program(stop_after=None, nseqs=SEQS):
    nc = bass.Bass("TRN2", target_bir_lowering=False)
    with contextlib.ExitStack() as st:
        b = Builder(nc, st)
        b.build(stop_after=stop_after, nseqs=nseqs)
    return nc


_NC_CACHE = None


def kernel(L0_attn_x, L0_attn_Wqk, L0_attn_bqk, L0_attn_Wv, L0_attn_bv,
           L0_ffn_Wr, L0_ffn_br, L0_ffn_W1, L0_ffn_b1, L0_ffn_W2, L0_ffn_b2,
           L0_ffn_gamma, L0_ffn_beta):
    global _NC_CACHE
    x = np.asarray(L0_attn_x, dtype=np.float32)
    Wqk = np.asarray(L0_attn_Wqk, np.float32).reshape(D, H, 2, DH)
    qpart = Wqk[:, :, 0, :].reshape(D, 8, 2 * DH)
    kpart = Wqk[:, :, 1, :].reshape(D, 8, 2 * DH)
    wqkp = np.ascontiguousarray(
        np.concatenate([qpart, kpart], axis=1).reshape(D, 2048))
    bqk = np.asarray(L0_attn_bqk, np.float32).reshape(H, 2, DH)
    bqkp = np.ascontiguousarray(np.concatenate(
        [bqk[:, 0, :].reshape(8, 128), bqk[:, 1, :].reshape(8, 128)],
        axis=0).reshape(2048))
    wv = np.ascontiguousarray(np.asarray(L0_attn_Wv, np.float32))
    bv = np.ascontiguousarray(np.asarray(L0_attn_bv, np.float32))
    wrt = np.ascontiguousarray(np.asarray(L0_ffn_Wr, np.float32).T)
    br = np.ascontiguousarray(np.asarray(L0_ffn_br, np.float32))
    w1t = np.ascontiguousarray(np.asarray(L0_ffn_W1, np.float32).transpose(0, 2, 1))
    b1p = np.ascontiguousarray(
        np.asarray(L0_ffn_b1, np.float32).reshape(E, 32, 128).transpose(2, 0, 1)
        .reshape(128, E * 32))
    w2t = np.ascontiguousarray(np.asarray(L0_ffn_W2, np.float32).transpose(0, 2, 1))
    b2 = np.ascontiguousarray(np.asarray(L0_ffn_b2, np.float32))
    gam = np.ascontiguousarray(np.asarray(L0_ffn_gamma, np.float32))
    bet = np.ascontiguousarray(np.asarray(L0_ffn_beta, np.float32))

    if _NC_CACHE is None:
        _NC_CACHE = build_program()
    nc = _NC_CACHE

    in_maps = []
    for core in range(8):
        shard = np.ascontiguousarray(x[2 * core: 2 * core + 2])
        in_maps.append(dict(x=shard, wqkp=wqkp, bqkp=bqkp, wv=wv, bv=bv,
                            wrt=wrt, br=br, w1t=w1t, b1p=b1p, w2t=w2t, b2=b2,
                            gam=gam, bet=bet))
    res = run_bass_kernel_spmd(nc, in_maps, core_ids=list(range(8)))
    out = np.concatenate([np.asarray(r["out"]).reshape(SEQS, S, D)
                          for r in res.results], axis=0)
    return out.astype(np.float32)


# version 13
# speedup vs baseline: 1.2548x; 1.0404x over previous
import contextlib
import numpy as np

import concourse.bass as bass
import concourse.mybir as mybir
from concourse.bass_utils import run_bass_kernel_spmd

F32 = mybir.dt.float32
BF16 = mybir.dt.bfloat16
I32 = mybir.dt.int32
U8 = mybir.dt.uint8
AF = mybir.ActivationFunctionType
ALU = mybir.AluOpType
AX = mybir.AxisListType

SEQS = 2
S = 2048
D = 1024
DK = 8
H = 16
DH = 64
E = 8
FF = 4096
NCH = 16
NT4 = 4
NQT = 8
VSTRIDE = 65
SHIFT = 16.0

ENGS = ("sync", "tensor", "scalar", "vector", "gpsimd")


class Ctr:
    def __init__(self, handle):
        self.h = handle
        self.n = 0

    def bump(self, k):
        self.n += k
        return self.n


class Prog:

    def __init__(self):
        self.ops = {e: [] for e in ENGS}

    def op(self, eng, fn, waits=(), inc=None, inc_k=1):
        w = [(c.h, v) for (c, v) in waits if v > 0]
        val = None
        if inc is not None:
            val = inc.bump(inc_k)
        self.ops[eng].append((fn, w, (inc.h, inc_k) if inc is not None else None))
        return val

    def replay(self, blk):
        def mk(eng_name):
            ops = self.ops[eng_name]

            def run(e):
                guaranteed = {}
                for fn, waits, inc in ops:
                    for (h, v) in waits:
                        if guaranteed.get(id(h), -1) >= v:
                            continue
                        e.wait_ge(h, v)
                        guaranteed[id(h)] = v
                    ins = fn(e)
                    if inc is not None:
                        h, k = inc
                        assert ins is not None
                        ins.then_inc(h, k)
            return run

        blk.sync(mk("sync"))
        blk.tensor(mk("tensor"))
        blk.scalar(mk("scalar"))
        blk.vector(mk("vector"))
        blk.gpsimd(mk("gpsimd"))


class Builder:
    def __init__(self, nc, st):
        self.nc = nc
        self.p = Prog()
        self.st = st

        self.x = nc.dram_tensor("x", [SEQS, S, D], F32, kind="ExternalInput").ap()
        self.wqkp = nc.dram_tensor("wqkp", [D, 2048], F32, kind="ExternalInput").ap()
        self.bqkp = nc.dram_tensor("bqkp", [2048], F32, kind="ExternalInput").ap()
        self.wv = nc.dram_tensor("wv", [D, D], F32, kind="ExternalInput").ap()
        self.bv = nc.dram_tensor("bv", [D], F32, kind="ExternalInput").ap()
        self.wrt = nc.dram_tensor("wrt", [D, E], F32, kind="ExternalInput").ap()
        self.br = nc.dram_tensor("br", [E], F32, kind="ExternalInput").ap()
        self.w1t = nc.dram_tensor("w1t", [E, D, FF], F32, kind="ExternalInput").ap()
        self.b1p = nc.dram_tensor("b1p", [128, E * 32], F32, kind="ExternalInput").ap()
        self.w2t = nc.dram_tensor("w2t", [E, FF, D], F32, kind="ExternalInput").ap()
        self.b2 = nc.dram_tensor("b2", [E, D], F32, kind="ExternalInput").ap()
        self.gam = nc.dram_tensor("gam", [D], F32, kind="ExternalInput").ap()
        self.bet = nc.dram_tensor("bet", [D], F32, kind="ExternalInput").ap()
        self.out = nc.dram_tensor("out", [SEQS, S, D], F32, kind="ExternalOutput").ap()
        self.x1s = nc.dram_tensor("x1s", [SEQS, S, D], F32, kind="Internal").ap()
        self.qscr = nc.dram_tensor("qscr", [SEQS, NQT, 128, 8, 256], BF16,
                                   kind="Internal").ap()

        sb = lambda name, shape, dt: st.enter_context(nc.sbuf_tensor(name, shape, dt))
        self.identf = sb("identf", [128, 128], F32)
        self.idi = sb("idi", [128, 128], I32)
        self.ones_bf = sb("ones_bf", [1, 512], BF16)
        self.ones_f = sb("ones_f", [1, 128], F32)
        self.bqk_bf = sb("bqk_bf", [1, 2048], BF16)
        self.bv_bf = sb("bv_bf", [1, D], BF16)
        self.b1_sb = sb("b1_sb", [128, E * 32], F32)
        self.b2_sb = sb("b2_sb", [E, D], F32)
        self.wrt_sb = sb("wrt_sb", [128, DK * E], F32)
        self.brb = sb("brb", [128, E], F32)
        self.gam_b = sb("gam_b", [128, D], F32)
        self.bet_b = sb("bet_b", [128, D], F32)
        self.router_bf = sb("router_bf", [E, S], BF16)
        self.b2_bf = sb("b2_bf", [E, D], BF16)
        self.er_sel_bf = sb("er_sel_bf", [E, E * 128], BF16)
        self.rbc = sb("rbc", [128, 2 * S], BF16)
        self.consts = sb("consts", [128, 8], F32)
        self.smalls = sb("smalls", [128, 64], F32)
        self.rout = sb("rout", [128, 40], F32)

        self.ARENA = 168448
        self.arena = st.enter_context(nc.sbuf_tensor("arena", [128, self.ARENA], U8))

        ps = lambda name: st.enter_context(nc.psum_tensor(name, [128, 512], F32))
        self.pA, self.pB = ps("pA"), ps("pB")
        self.pC, self.pD = ps("pC"), ps("pD")
        self.pE, self.pF = ps("pE"), ps("pF")

        sem = lambda name: st.enter_context(nc.semaphore(name))
        self.st_ = Ctr(sem("s_t"))
        self.sc_ = Ctr(sem("s_c"))
        self.sv_ = Ctr(sem("s_v"))
        self.sg_ = Ctr(sem("s_g"))
        self.qs_ = Ctr(sem("q_s"))
        self.qx_ = Ctr(sem("q_x"))
        self.qd_ = Ctr(sem("q_d"))
        self.qq_ = Ctr(sem("q_q"))
        self.qxd_ = Ctr(sem("q_xd"))
        self.qo_ = Ctr(sem("q_o"))
        self.qg_ = Ctr(sem("q_g"))
        self.qk0_ = Ctr(sem("q_k0"))
        self.qk1_ = Ctr(sem("q_k1"))
        self.wv0_ = Ctr(sem("q_v0"))
        self.wv1_ = Ctr(sem("q_v1"))
        self.qf_ = Ctr(sem("q_f"))
        self.qt0_ = Ctr(sem("q_t0"))
        self.qt1_ = Ctr(sem("q_t1"))

        self.o_kfm = 0
        self.o_vaug = 32768
        self.o_acc = 66048
        SCR = 131584
        self.SCR = SCR
        self.o_xt = 66048
        self.o_sq_p = 82432
        self.o_ln1bf = 86528
        self.o_win = 90624
        self.o_qwin = 107008
        self.o_wqst = 115200
        self.o_wqbf = 123392
        self.o_wvst = SCR
        self.o_wvbf = SCR + 16384
        self.o_pt = SCR
        self.o_xch2 = SCR + 2048
        self.o_attn = SCR + 8192
        self.o_xch = SCR + 16384
        self.o_tgts = SCR + 20480
        self.o_qqt = SCR + 28672
        self.o_w1b = 0
        self.o_w2b = 32768
        self.o_hblk = 49152
        self.o_wst = 57344
        self.o_tgtfm = SCR
        self.o_hrelu = SCR + 32768
        self.o_otm = 0
        self.o_xo = 4096
        self.o_tmp = 8192
        self.o_sq_o = 12288

    def carve(self, off, n, dt, parts=128):
        nb = n * mybir.dt.size(dt)
        assert off + nb <= self.ARENA, (off, nb)
        return self.arena[0:parts, off:off + nb].bitcast(dt)

    def mm(self, bank, lhsT, rhs, start, stop, waits=(), inc=None):
        return self.p.op(
            "tensor",
            lambda e, b=bank, l=lhsT, r=rhs, s0=start, s1=stop: e.matmul(
                b, l, r, start=s0, stop=s1),
            waits=waits, inc=inc)

    def setup(self):
        p = self.p
        v, sc, g, q = "vector", "scalar", "gpsimd", "sync"
        stage1 = self.carve(0, 2048, F32, parts=1)
        p.op(v, lambda e: e.memset(self.ones_bf[:], 1.0))
        p.op(v, lambda e: e.memset(self.ones_f[:], 1.0))
        p.op(v, lambda e: e.memset(self.consts[:, 0:1], -SHIFT))
        p.op(v, lambda e: e.memset(self.consts[:, 1:2], 1e-5), inc=self.sv_)
        ii = p.op(g, lambda e: e.iota(self.idi[:], [[1, 128]], base=0,
                                      channel_multiplier=-1), inc=self.sg_)
        i2 = p.op(v, lambda e: e.tensor_scalar(self.identf[:], self.idi[:], 0, None,
                                               op0=ALU.is_equal),
                  waits=[(self.sg_, ii)], inc=self.sv_)
        p.op(v, lambda e: e.tensor_copy(
            self.er_sel_bf[:].rearrange("p (j w) -> p j w", w=128),
            self.identf[0:E, 0:E].unsqueeze(2).broadcast_to((E, E, 128))),
            waits=[(self.sv_, i2)], inc=self.sv_)
        p.op(q, lambda e: e.dma_start(
            self.wrt_sb[:].rearrange("p (dk e) -> p dk e", e=E),
            self.wrt.rearrange("(dk p) e -> p dk e", p=128)), inc=self.qs_, inc_k=16)
        p.op(q, lambda e: e.dma_start(self.b1_sb[:], self.b1p), inc=self.qs_, inc_k=16)
        qb2 = p.op(q, lambda e: e.dma_start(self.b2_sb[:], self.b2),
                   inc=self.qs_, inc_k=16)
        p.op(v, lambda e: e.tensor_copy(self.b2_bf[:], self.b2_sb[:]),
             waits=[(self.qs_, qb2)], inc=self.sv_)
        q2 = p.op(q, lambda e: e.dma_start(
            stage1, self.bqkp.rearrange("(one n) -> one n", one=1)),
            inc=self.qs_, inc_k=16)
        c1 = p.op(v, lambda e: e.tensor_copy(self.bqk_bf[:], stage1),
                  waits=[(self.qs_, q2)], inc=self.sv_)
        q3 = p.op(q, lambda e: e.dma_start(
            stage1[0:1, 0:D], self.bv.rearrange("(one n) -> one n", one=1)),
            waits=[(self.sv_, c1)], inc=self.qs_, inc_k=16)
        c2 = p.op(v, lambda e: e.tensor_copy(self.bv_bf[:], stage1[0:1, 0:D]),
                  waits=[(self.qs_, q3)], inc=self.sv_)
        q4 = p.op(q, lambda e: e.dma_start(
            stage1[0:1, 0:D], self.gam.rearrange("(one n) -> one n", one=1)),
            waits=[(self.sv_, c2)], inc=self.qs_, inc_k=16)
        pE_busy = 0
        for half in range(2):
            t1 = self.mm(self.pE[:, 0:512], self.ones_f[0:1, 0:128],
                         stage1[0:1, half * 512: half * 512 + 512],
                         True, True, waits=[(self.qs_, q4),
                                            (self.sc_, pE_busy)], inc=self.st_)
            pE_busy = p.op(sc, lambda e, h=half: e.copy(
                self.gam_b[:, h * 512: h * 512 + 512], self.pE[:, 0:512]),
                waits=[(self.st_, t1)], inc=self.sc_)
        q5 = p.op(q, lambda e: e.dma_start(
            stage1[0:1, 0:D], self.bet.rearrange("(one n) -> one n", one=1)),
            waits=[(self.sc_, pE_busy)], inc=self.qs_, inc_k=16)
        for half in range(2):
            t1 = self.mm(self.pE[:, 0:512], self.ones_f[0:1, 0:128],
                         stage1[0:1, half * 512: half * 512 + 512],
                         True, True, waits=[(self.qs_, q5),
                                            (self.sc_, pE_busy)], inc=self.st_)
            pE_busy = p.op(sc, lambda e, h=half: e.copy(
                self.bet_b[:, h * 512: h * 512 + 512], self.pE[:, 0:512]),
                waits=[(self.st_, t1)], inc=self.sc_)
        q6 = p.op(q, lambda e: e.dma_start(
            stage1[0:1, 0:E], self.br.rearrange("(one n) -> one n", one=1)),
            waits=[(self.sc_, pE_busy)], inc=self.qs_, inc_k=16)
        t1 = self.mm(self.pE[:, 0:8], self.ones_f[0:1, 0:128],
                     stage1[0:1, 0:E], True, True,
                     waits=[(self.qs_, q6), (self.sc_, pE_busy)], inc=self.st_)
        p.op(sc, lambda e: e.copy(self.brb[:], self.pE[:, 0:8]),
             waits=[(self.st_, t1)], inc=self.sc_)

    def ln_stats(self, src_ap_fn, width, eng_waits, lane=0):
        p = self.p
        base = lane * 16
        bn = self.smalls[:, base:base + 12]
        mv = self.smalls[:, base + 12:base + 14]
        lnv = self.smalls[:, base + 14:base + 15]
        rstd = self.smalls[:, base + 15:base + 16]
        hw = width // 2
        entry = [(self.sv_, self.sv_.n)]
        p.op("vector", lambda e: e.bn_stats(bn[:, 0:6], src_ap_fn()[:, 0:hw]),
             waits=list(eng_waits) + entry, inc=self.sv_)
        a1 = p.op("vector", lambda e: e.bn_stats(bn[:, 6:12],
                                                 src_ap_fn()[:, hw:width]),
                  waits=list(eng_waits), inc=self.sv_)
        a2 = p.op("vector", lambda e: e.bn_aggr(mv, bn),
                  waits=[(self.sv_, a1)], inc=self.sv_)
        b1 = p.op("scalar", lambda e: e.activation(
            lnv, mv[:, 1:2], AF.Ln, bias=self.consts[:, 1:2], scale=1.0),
            waits=[(self.sv_, a2)], inc=self.sc_)
        b2 = p.op("scalar", lambda e: e.activation(rstd, lnv, AF.Exp, scale=-0.5),
                  waits=[(self.sc_, b1)], inc=self.sc_)
        return mv[:, 0:1], rstd, b2

    def stage_p(self, s):
        p = self.p
        kfm = self.carve(self.o_kfm, 8 * 2048, BF16)
        vaug = self.carve(self.o_vaug, 16 * 1040, BF16)
        von = p.op("vector", lambda e: e.memset(
            vaug.rearrange("p (c h z) -> p c h z", h=16, z=VSTRIDE)[:, :, :, 64:65],
            1.0), inc=self.sv_)
        self.vaug_ones = von

        lastln = 0
        win_reads = [0, 0]
        ln1_reads = [0, 0]
        qw_dump = 0
        wq_cast = {}
        last_w_mm = {}
        bankAB_evac = [self.sc_.n, self.sc_.n]
        bankCD_evac = [self.sc_.n, self.sc_.n]
        self.kfm_done_sc = 0
        self.vaug_done_sc = 0
        self.qdump_qs = 0

        for t4 in range(NT4):
            par = t4 % 2
            xt = self.carve(self.o_xt, 4 * 1024, F32)
            qx = p.op("sync", lambda e, s=s, t4=t4, xt=xt: e.dma_start(
                xt.rearrange("p (c d) -> p c d", d=D),
                self.x[s, t4 * 512:(t4 + 1) * 512, :].rearrange(
                    "(c p) d -> p c d", p=128)),
                waits=[(self.sv_, lastln)], inc=self.qx_, inc_k=16)
            tr_done = 0
            for c in range(4):
                cpar = c % 2
                xt_c = lambda xt=xt, c=c: xt[:, c * 1024:(c + 1) * 1024]
                m, rstd, bdone = self.ln_stats(xt_c, D, [(self.qx_, qx)])
                ln1bf = self.carve(self.o_ln1bf + cpar * 2048, 1024, BF16)
                a6 = p.op("vector", lambda e, xt_c=xt_c, m=m, rstd=rstd,
                          ln1bf=ln1bf: e.scalar_tensor_tensor(
                              ln1bf, xt_c(), m, rstd.broadcast_to((128, 1024)),
                              op0=ALU.subtract, op1=ALU.mult),
                          waits=[(self.sc_, bdone),
                                 ((self.qt0_, self.qt1_)[cpar], ln1_reads[cpar])],
                          inc=self.sv_)
                lastln = a6
                wdst = self.carve(self.o_win + par * 8192, 8 * 512, BF16)
                qtc = (self.qt0_, self.qt1_)[cpar]
                for dk in range(DK):
                    p.op("scalar", lambda e, wdst=wdst, dk=dk, c=c,
                         ln1bf=ln1bf: e.dma_start_transpose(
                             wdst[:, dk * 512 + c * 128: dk * 512 + c * 128 + 128],
                             ln1bf[:, dk * 128:(dk + 1) * 128]),
                         waits=[(self.sv_, a6), (self.st_, win_reads[par])],
                         inc=qtc, inc_k=16)
                ln1_reads[cpar] = qtc.n
            tr_w = [(self.qt0_, self.qt0_.n), (self.qt1_, self.qt1_.n)]
            win = self.carve(self.o_win + par * 8192, 8 * 512, BF16)
            qwin = self.carve(self.o_qwin, 8 * 512, BF16)
            for oc in range(16):
                po = oc % 2
                wqst = self.carve(self.o_wqst + po * 4096, 8 * 128, F32)
                wqbf = self.carve(self.o_wqbf + po * 2048, 8 * 128, BF16)
                qsem = (self.qk0_, self.qk1_)[po]
                qd = p.op("sync", lambda e, oc=oc, wqst=wqst: e.dma_start(
                    wqst.rearrange("p (dk o) -> p dk o", o=128),
                    self.wqkp[:, oc * 128:(oc + 1) * 128].rearrange(
                        "(dk p) o -> p dk o", p=128)),
                    waits=[(self.sv_, wq_cast.get(("qk", po), 0))],
                    inc=qsem, inc_k=16)
                cc = p.op("vector", lambda e, wqbf=wqbf, wqst=wqst: e.tensor_copy(
                    wqbf, wqst),
                    waits=[(qsem, qd), (self.st_, last_w_mm.get(("qk", po), 0))],
                    inc=self.sv_)
                wq_cast[("qk", po)] = cc
                bank = (self.pA, self.pB)[oc % 2]
                for dk in range(DK):
                    self.mm(bank[:, 0:512],
                            wqbf[:, dk * 128:(dk + 1) * 128],
                            win[:, dk * 512:(dk + 1) * 512],
                            start=(dk == 0), stop=False,
                            waits=(([(self.sv_, cc),
                                     (self.sc_, bankAB_evac[oc % 2])] + tr_w)
                                   if dk == 0 else ()))
                tmm = self.mm(bank[:, 0:512],
                              self.bqk_bf[0:1, oc * 128:(oc + 1) * 128],
                              self.ones_bf[0:1, 0:512], start=False, stop=True,
                              inc=self.st_)
                last_w_mm[("qk", po)] = tmm
                if oc < 8:
                    dst = qwin.rearrange("p (j o t) -> p j o t",
                                         j=2, o=8)[:, :, oc, :]
                    w_extra = [(self.qd_, qw_dump)]
                    ev = p.op("scalar", lambda e, dst=dst, bank=bank:
                              e.activation(
                                  dst,
                                  bank[:, 0:512].rearrange("p (j t) -> p j t",
                                                           t=256),
                                  AF.Copy, scale=1.0 / 32.0),
                              waits=[(self.st_, tmm)] + w_extra, inc=self.sc_)
                else:
                    dst = kfm[:, (oc - 8) * 2048 + t4 * 512:
                              (oc - 8) * 2048 + t4 * 512 + 512]
                    ev = p.op("scalar", lambda e, dst=dst, bank=bank:
                              e.activation(
                                  dst, bank[:, 0:512], AF.Copy,
                                  scale=1.0 / 32.0),
                              waits=[(self.st_, tmm)], inc=self.sc_)
                bankAB_evac[oc % 2] = ev
                if oc == 15:
                    self.kfm_done_sc = ev
            qev = self.sc_.n
            for j in range(2):
                qw_dump = p.op("sync", lambda e, s=s, t4=t4, j=j, qwin=qwin:
                               e.dma_start(
                                   self.qscr[s, 2 * t4 + j],
                                   qwin[:, j * 2048:(j + 1) * 2048].rearrange(
                                       "p (o t) -> p o t", t=256)),
                               waits=[(self.sc_, qev)], inc=self.qd_, inc_k=16)
            self.qdump_qs = qw_dump
            for vc in range(4):
                pv = vc % 2
                wvst = self.carve(self.o_wvst + pv * 8192, 8 * 256, F32)
                wvbf = self.carve(self.o_wvbf + pv * 4096, 8 * 256, BF16)
                vsem = (self.wv0_, self.wv1_)[pv]
                qd = p.op("sync", lambda e, vc=vc, wvst=wvst: e.dma_start(
                    wvst.rearrange("p (dk o) -> p dk o", o=256),
                    self.wv[:, vc * 256:(vc + 1) * 256].rearrange(
                        "(dk p) o -> p dk o", p=128)),
                    waits=[(self.sv_, wq_cast.get(("v", pv), 0))],
                    inc=vsem, inc_k=16)
                cc = p.op("vector", lambda e, wvbf=wvbf, wvst=wvst: e.tensor_copy(
                    wvbf, wvst),
                    waits=[(vsem, qd), (self.st_, last_w_mm.get(("v", pv), 0))],
                    inc=self.sv_)
                wq_cast[("v", pv)] = cc
                for c in range(4):
                    bpar = (vc * 4 + c) % 2
                    bank = (self.pC, self.pD)[bpar]
                    for dk in range(DK):
                        self.mm(bank[:, 0:256],
                                win[:, dk * 512 + c * 128: dk * 512 + c * 128 + 128],
                                wvbf[:, dk * 256:(dk + 1) * 256],
                                start=(dk == 0), stop=False,
                                waits=(([(self.sv_, cc),
                                         (self.sc_, bankCD_evac[bpar])] + tr_w)
                                       if dk == 0 else ()))
                    tmm = self.mm(bank[:, 0:256], self.ones_bf[0:1, 0:128],
                                  self.bv_bf[0:1, vc * 256:(vc + 1) * 256],
                                  start=False, stop=True, inc=self.st_)
                    last_w_mm[("v", pv)] = tmm
                    gch = t4 * 4 + c
                    dst = vaug[:, gch * 1040: (gch + 1) * 1040].rearrange(
                        "p (h z) -> p h z", z=VSTRIDE)[:, vc * 4:(vc + 1) * 4, 0:64]
                    ev = p.op("scalar", lambda e, dst=dst, bank=bank: e.activation(
                        dst, bank[:, 0:256].rearrange("p (h w) -> p h w", w=64),
                        AF.Copy, scale=1.0 / 32.0),
                        waits=[(self.st_, tmm), (self.sv_, von)], inc=self.sc_)
                    bankCD_evac[bpar] = ev
            win_reads[par] = self.st_.n
            self.vaug_done_sc = self.sc_.n
        self.stageP_end = dict(sv=self.sv_.n, sc=self.sc_.n, st=self.st_.n,
                               sg=self.sg_.n)

    def stage_a(self, s):
        p = self.p
        kfm = self.carve(self.o_kfm, 8 * 2048, BF16)
        vaug = self.carve(self.o_vaug, 16 * 1040, BF16)
        acc = self.carve(self.o_acc, 8 * 2048, F32)
        pt = self.carve(self.o_pt, 2 * 512, BF16)
        attn = self.carve(self.o_attn, 2 * 1024, F32)

        pt_war = [0, 0]
        entryA = [(self.st_, self.stageP_end["st"]),
                  (self.sg_, self.stageP_end["sg"]),
                  (self.sc_, self.stageP_end["sc"]),
                  (self.sv_, self.stageP_end["sv"])]
        qreload_war = 0
        qchain = 0
        bankAB_evac = [self.stageP_end["sc"], self.stageP_end["sc"]]
        bankCD_free = [self.stageP_end["sc"], self.stageP_end["sc"]]
        attn_war = [0, 0]
        xch_war = [0, 0]
        xch_war_qs = [0, 0]
        tgts_war = [0, 0]
        pv_stop = [0, 0]
        pending_back = None
        pEF_free = [0, 0]
        self.acc_sc = 0
        self.router_sc = 0
        rs0 = self.smalls[:, 40:41]
        rs1 = self.smalls[:, 41:42]

        qqt0 = self.carve(self.o_qqt, 8 * 256, BF16)
        qr_val = [0, 0]
        qr_val[0] = p.op("gpsimd", lambda e, s=s, qqt0=qqt0: e.dma_start(
            qqt0.rearrange("p (oc t) -> p oc t", t=256), self.qscr[s, 0]),
            waits=[(self.qd_, self.qdump_qs)], inc=self.qq_, inc_k=16)
        for qt in range(NQT):
            qq = qt % 2
            qqt = self.carve(self.o_qqt + qq * 4096, 8 * 256, BF16)
            qr = qr_val[qq]
            qt_scores_start = self.st_.n
            for h in range(H):
                hp = (h % 2) * 64
                hc = h // 2
                evs = {}

                def emit_pv(kp, h=h):
                    nonlocal pv_stop
                    ptb = pt[:, (kp % 2) * 512:(kp % 2) * 512 + 512]
                    ev = evs[kp]
                    for half in range(2):
                        kc = 2 * kp + half
                        for q2 in range(2):
                            bank = (self.pC, self.pD)[q2]
                            if half == 0 and q2 == 0:
                                w = [(self.sc_, ev)]
                                if kc == 0:
                                    w += [(self.sc_, bankCD_free[0]),
                                          (self.sc_, bankCD_free[1]),
                                          (self.sc_, self.vaug_done_sc),
                                          (self.sv_, self.vaug_ones)]
                            else:
                                w = ()
                            self.mm(
                                bank[:, 0:65],
                                ptb[:, half * 256 + q2 * 128:
                                    half * 256 + q2 * 128 + 128],
                                vaug[:, kc * 1040 + h * VSTRIDE:
                                     kc * 1040 + h * VSTRIDE + VSTRIDE],
                                start=(kc == 0), stop=(kc == NCH - 1),
                                waits=w,
                                inc=(self.st_ if kc == NCH - 1 else None))
                    pt_war[kp % 2] = self.st_.n
                    if kp == 7:
                        pv_stop = [self.st_.n - 1, self.st_.n]

                for kp in range(8):
                    sbank = (self.pA, self.pB)[kp % 2]
                    ptb = pt[:, (kp % 2) * 512:(kp % 2) * 512 + 512]
                    for half in range(2):
                        kc = 2 * kp + half
                        tmm = self.mm(
                            sbank[:, half * 256: half * 256 + 256],
                            kfm[hp:hp + 64,
                                hc * 2048 + kc * 128: hc * 2048 + kc * 128 + 128],
                            qqt[hp:hp + 64, hc * 256: hc * 256 + 256],
                            start=True, stop=True,
                            waits=([(self.qq_, qr),
                                    (self.sc_, bankAB_evac[kp % 2]),
                                    (self.st_, pt_war[kp % 2]),
                                    (self.sc_, self.kfm_done_sc)]
                                   if half == 0 else ()),
                            inc=(self.st_ if half == 1 else None))
                        if h == 0 and kp == 0 and half == 0:
                            qchain = self.st_.n + 1
                    ev = p.op("scalar", lambda e, sbank=sbank, ptb=ptb:
                              e.activation(
                                  ptb, sbank[:, 0:512],
                                  AF.Exp, bias=self.consts[:, 0:1],
                                  scale=1.0 / 64.0),
                              waits=[(self.st_, tmm)] + entryA, inc=self.sc_)
                    bankAB_evac[kp % 2] = ev
                    evs[kp] = ev
                    if kp >= 1:
                        emit_pv(kp - 1)
                emit_pv(7)
                for q2 in range(2):
                    bank = (self.pC, self.pD)[q2]
                    tpv = pv_stop[q2]
                    rs = (rs0, rs1)[q2]
                    rv = p.op("vector", lambda e, bank=bank, rs=rs: e.reciprocal(
                        rs, bank[:, 64:65]),
                        waits=[(self.st_, tpv)], inc=self.sv_)
                    ev = p.op("scalar", lambda e, bank=bank, rs=rs, h=h, q2=q2,
                              attn=attn: e.activation(
                                  attn[:, q2 * 1024 + h * 64:
                                       q2 * 1024 + h * 64 + 64],
                                  bank[:, 0:64], AF.Copy, scale=rs),
                              waits=[(self.sv_, rv), (self.sv_, attn_war[q2])]
                                    + entryA,
                              inc=self.sc_)
                    bankCD_free[q2] = ev
            qreload_war = qt_scores_start
            attn_done = self.sc_.n
            if qt + 1 < NQT:
                nqq = (qt + 1) % 2
                nqqt = self.carve(self.o_qqt + nqq * 4096, 8 * 256, BF16)
                qr_val[nqq] = p.op(
                    "gpsimd", lambda e, s=s, qt=qt, nqqt=nqqt: e.dma_start(
                        nqqt.rearrange("p (oc t) -> p oc t", t=256),
                        self.qscr[s, qt + 1]),
                    waits=[(self.qd_, self.qdump_qs),
                           (self.st_, qreload_war)],
                    inc=self.qq_, inc_k=16)
            if pending_back is not None:
                pending_back()
                pending_back = None
            front_state = []
            xchs = [self.carve(self.o_xch, 1024, F32),
                    self.carve(self.o_xch2, 1024, F32)]
            tgtss = [self.carve(self.o_tgts + q2 * 4096, 1024, F32)
                     for q2 in range(2)]
            attn_cs = [
                (lambda attn=attn, q2=q2: attn[:, q2 * 1024:(q2 + 1) * 1024])
                for q2 in range(2)]
            sA = [self.ln_stats(attn_cs[q2], D, [(self.sc_, attn_done)],
                                lane=q2) for q2 in range(2)]
            qxs = []
            for q2 in range(2):
                g = qt * 2 + q2
                qxs.append(p.op(
                    "sync", lambda e, s=s, g=g, xch=xchs[q2]: e.dma_start(
                        xch, self.x[s, g * 128:(g + 1) * 128, :]),
                    waits=[(self.sv_, xch_war[q2]),
                           (self.qxd_, xch_war_qs[q2])] + entryA,
                    inc=self.qx_, inc_k=16))
            a1s = []
            for q2 in range(2):
                m, rstd, bdone = sA[q2]
                a1 = p.op("vector", lambda e, attn_c=attn_cs[q2], m=m,
                          rstd=rstd, tgts=tgtss[q2]: e.scalar_tensor_tensor(
                              tgts, attn_c(), m, rstd.broadcast_to((128, D)),
                              op0=ALU.subtract, op1=ALU.mult),
                          waits=[(self.sc_, bdone), (self.st_, tgts_war[q2])]
                                + entryA,
                          inc=self.sv_)
                attn_war[q2] = a1
                a1s.append(a1)
            a2s = []
            for q2 in range(2):
                a2s.append(p.op(
                    "vector", lambda e, xch=xchs[q2], tgts=tgtss[q2]:
                    e.tensor_tensor(xch, xch, tgts, op=ALU.add),
                    waits=[(self.sv_, a1s[q2]), (self.qx_, qxs[q2])],
                    inc=self.sv_))
            sB = [self.ln_stats(lambda xch=xchs[q2]: xch[:, 0:D], D,
                                [(self.sv_, a2s[q2])], lane=q2)
                  for q2 in range(2)]
            for q2 in range(2):
                g = qt * 2 + q2
                m2, rstd2, b2d = sB[q2]
                a3 = p.op("vector", lambda e, xch=xchs[q2], m2=m2,
                          rstd2=rstd2, tgts=tgtss[q2]: e.scalar_tensor_tensor(
                              tgts, xch, m2, rstd2.broadcast_to((128, D)),
                              op0=ALU.subtract, op1=ALU.mult),
                          waits=[(self.sc_, b2d)], inc=self.sv_)
                qd = p.op("sync", lambda e, s=s, g=g, xch=xchs[q2]:
                          e.dma_start(
                              self.x1s[s, g * 128:(g + 1) * 128, :], xch),
                          waits=[(self.sv_, a2s[q2])], inc=self.qxd_, inc_k=16)
                xch_war[q2] = a3
                xch_war_qs[q2] = qd
                front_state.append((g, q2, a3, tgtss[q2]))

            def back(fs=tuple(front_state)):
                for (g, q2, a3, tgts) in fs:
                    for dk in range(DK):
                        bank = (self.pE, self.pF)[dk % 2]
                        t1 = p.op("tensor", lambda e, bank=bank, tgts=tgts,
                                  dk=dk: e.transpose(
                                      bank[:, 0:128],
                                      tgts[:, dk * 128:(dk + 1) * 128],
                                      self.identf[:]),
                                  waits=[(self.sv_, a3),
                                         (self.sc_, pEF_free[dk % 2])],
                                  inc=self.st_)
                        cp = p.op("scalar", lambda e, bank=bank, dk=dk, g=g,
                                  acc=acc: e.copy(
                                      acc[:, dk * 2048 + g * 128:
                                          dk * 2048 + g * 128 + 128],
                                      bank[:, 0:128]),
                                  waits=[(self.st_, t1)], inc=self.sc_)
                        pEF_free[dk % 2] = cp
                    tgts_war[q2] = self.st_.n
                    acc_done = self.sc_.n
                    self.acc_sc = acc_done
                    for dk in range(DK):
                        self.mm(self.pF[:, 0:8],
                                acc[:, dk * 2048 + g * 128:
                                    dk * 2048 + g * 128 + 128],
                                self.wrt_sb[:, dk * 8:(dk + 1) * 8],
                                start=(dk == 0), stop=(dk == DK - 1),
                                waits=([(self.sc_, acc_done)] if dk == 0 else ()),
                                inc=(self.st_ if dk == DK - 1 else None))
                    tlg = self.st_.n
                    self.emit_router_chain(g, tlg)
                    pEF_free[1] = max(pEF_free[1], self.router_pf_free)
            pending_back = back
        if pending_back is not None:
            pending_back()
            pending_back = None
        self.x1dump_qxd = self.qxd_.n
        self.stageA_end = dict(sv=self.sv_.n, sc=self.sc_.n, st=self.st_.n,
                               sg=self.sg_.n)

    def emit_router_chain(self, g, tlg):
        p = self.p
        lg = self.rout[:, 0:8]
        d = self.rout[:, 8:16]
        mask = self.rout[:, 16:24]
        d2 = self.rout[:, 24:32]
        w = self.rout[:, 32:40]
        m1 = self.smalls[:, 32:33]
        m2 = self.smalls[:, 33:34]
        e2 = self.smalls[:, 34:35]
        sS = self.smalls[:, 35:36]
        rs = self.smalls[:, 36:37]
        entry = [(self.sv_, self.sv_.n)]
        a1 = p.op("vector", lambda e: e.tensor_tensor(lg, self.pF[:, 0:8],
                                                      self.brb[:], op=ALU.add),
                  waits=[(self.st_, tlg)] + entry, inc=self.sv_)
        a2 = p.op("vector", lambda e: e.reduce_max(m1, lg, axis=AX.X),
                  waits=[(self.sv_, a1)], inc=self.sv_)
        a3 = p.op("vector", lambda e: e.tensor_scalar(d, lg, m1, None,
                                                      op0=ALU.subtract),
                  waits=[(self.sv_, a2)], inc=self.sv_)
        a4 = p.op("vector", lambda e: e.tensor_scalar(mask, d, 0.0, None,
                                                      op0=ALU.is_equal),
                  waits=[(self.sv_, a3)], inc=self.sv_)
        a5 = p.op("vector", lambda e: e.scalar_tensor_tensor(
            d2, mask, -1e9, d, op0=ALU.mult, op1=ALU.add),
            waits=[(self.sv_, a4)], inc=self.sv_)
        a6 = p.op("vector", lambda e: e.reduce_max(m2, d2, axis=AX.X),
                  waits=[(self.sv_, a5)], inc=self.sv_)
        b1 = p.op("scalar", lambda e: e.activation(e2, m2, AF.Exp, scale=1.0),
                  waits=[(self.sv_, a6)], inc=self.sc_)
        a7 = p.op("vector", lambda e: e.tensor_scalar(sS, e2, 1.0, None,
                                                      op0=ALU.add),
                  waits=[(self.sc_, b1)], inc=self.sv_)
        a8 = p.op("vector", lambda e: e.reciprocal(rs, sS),
                  waits=[(self.sv_, a7)], inc=self.sv_)
        a9 = p.op("vector", lambda e: e.tensor_scalar(mask, d, m2, None,
                                                      op0=ALU.is_ge),
                  waits=[(self.sv_, a6)], inc=self.sv_)
        b2 = p.op("scalar", lambda e: e.activation(lg, d, AF.Exp, scale=1.0),
                  waits=[(self.sv_, a9)], inc=self.sc_)
        a10 = p.op("vector", lambda e: e.scalar_tensor_tensor(
            w, lg, rs, mask, op0=ALU.mult, op1=ALU.mult),
            waits=[(self.sc_, b2), (self.sv_, a8)], inc=self.sv_)
        t1 = p.op("tensor", lambda e: e.transpose(
            self.pF[0:8, 128:256], w, self.identf[:]),
            waits=[(self.sv_, a10)], inc=self.st_)
        c1 = p.op("scalar", lambda e, g=g: e.copy(
            self.router_bf[0:8, g * 128:(g + 1) * 128], self.pF[0:8, 128:256]),
            waits=[(self.st_, t1)], inc=self.sc_)
        self.router_pf_free = c1
        self.router_sc = c1

    def stage_f(self, s):
        p = self.p
        acc = self.carve(self.o_acc, 8 * 2048, F32)
        tgtfm = self.carve(self.o_tgtfm, 8 * 2048, BF16)
        hblk = self.carve(self.o_hblk, 8 * 512, BF16)
        A_end = self.stageA_end
        kv_free = [(self.st_, A_end["st"]), (self.sc_, A_end["sc"]),
                   (self.sv_, A_end["sv"]), (self.qxd_, self.x1dump_qxd),
                   (self.qx_, self.qx_.n), (self.qq_, self.qq_.n)]

        cast_done = 0
        for dk in range(DK):
            cast_done = p.op("vector", lambda e, dk=dk, acc=acc, tgtfm=tgtfm:
                             e.tensor_copy(tgtfm[:, dk * 2048:(dk + 1) * 2048],
                                           acc[:, dk * 2048:(dk + 1) * 2048]),
                             waits=[(self.sc_, self.acc_sc)] + kv_free,
                             inc=self.sv_)
        self.tgtfm_sv = cast_done

        pEF_free = [A_end["sc"], A_end["sc"]]
        pEF_free_v = [0, 0]
        for tt in range(4):
            for dk in range(DK):
                bp = dk % 2
                bank = (self.pE, self.pF)[bp]
                tmm = self.mm(bank[:, 0:512],
                              self.b2_bf[0:8, dk * 128:(dk + 1) * 128],
                              self.router_bf[0:8, tt * 512:(tt + 1) * 512],
                              start=True, stop=True,
                              waits=[(self.sc_, self.router_sc),
                                     (self.sv_, pEF_free_v[bp]),
                                     (self.sv_, cast_done)],
                              inc=self.st_)
                av = p.op("vector", lambda e, bank=bank, dk=dk, tt=tt, acc=acc:
                          e.tensor_tensor(
                              acc[:, dk * 2048 + tt * 512:
                                  dk * 2048 + tt * 512 + 512],
                              bank[:, 0:512],
                              acc[:, dk * 2048 + tt * 512:
                                  dk * 2048 + tt * 512 + 512],
                              op=ALU.add),
                          waits=[(self.st_, tmm)], inc=self.sv_)
                pEF_free_v[bp] = av

        lane_buf = [self.carve(self.o_wst + l * 2048, 512, F32)
                    for l in range(4)]
        lane_q = [self.qf_, self.qg_, self.qf_, self.qg_]
        lane_eng = ["sync", "gpsimd", "sync", "gpsimd"]
        lane_cast = [None, None, None, None]

        st_box = dict(w1_mm_war=[A_end["st"], A_end["st"]],
                      w2_mm_war=A_end["st"])
        rbc_sc = [0, 0]
        rbc_war_sv = [0, 0]
        bankAB_evac = [A_end["sc"], A_end["sc"]]
        bankCD_free = [max(pEF_free_v), max(pEF_free_v)]
        hrelu_war = [0, 0]
        hblk_war = [A_end["st"]]
        pE_free_sc = [A_end["sc"]]
        pE_free_v = max(pEF_free_v)
        h_last_sv = [0]

        blocks = [(e_, fb) for e_ in range(E) for fb in range(4)]
        NB = len(blocks)
        w1c = [[0, 0] for _ in range(NB)]
        w2c = [[0, 0] for _ in range(NB)]

        w1_qd = {}
        w2_qd = {}

        def _load(qd_map, i, k, src_fn):
            l = k % 4
            w = [lane_cast[l]] if lane_cast[l] is not None else []
            qd_map[(i, k)] = p.op(
                lane_eng[l], lambda e, src_fn=src_fn, l=l:
                e.dma_start(lane_buf[l], src_fn()),
                waits=w + kv_free, inc=lane_q[l], inc_k=16)

        def _cast(qd_map, i, k, dst, war, cmax):
            l = k % 4
            ve = True
            ctr = self.sv_ if ve else self.sc_
            fn = ((lambda e, dst=dst, l=l: e.tensor_copy(dst, lane_buf[l]))
                  if ve else
                  (lambda e, dst=dst, l=l: e.copy(dst, lane_buf[l])))
            cc = p.op("vector" if ve else "scalar", fn,
                      waits=[(lane_q[l], qd_map[(i, k)]), (self.st_, war)],
                      inc=ctr)
            lane_cast[l] = (ctr, cc)
            idx = 0 if ve else 1
            cmax[i][idx] = max(cmax[i][idx], cc)

        def w1_load(i, k):
            e_, fb = blocks[i]
            dk, c = k // 2, k % 2
            _load(w1_qd, i, k, lambda e_=e_, fb=fb, dk=dk, c=c:
                  self.w1t[e_, dk * 128:(dk + 1) * 128,
                           fb * 1024 + c * 512: fb * 1024 + (c + 1) * 512])

        def w1_cast(i, k):
            dk, c = k // 2, k % 2
            w1b = self.carve(self.o_w1b + (i % 2) * 16384, 8 * 1024, BF16)
            _cast(w1_qd, i, k,
                  w1b[:, dk * 1024 + c * 512: dk * 1024 + (c + 1) * 512],
                  st_box["w1_mm_war"][i % 2], w1c)

        def w2_load(i, k):
            e_, fb = blocks[i]
            fc, c = k // 2, k % 2
            _load(w2_qd, i, k, lambda e_=e_, fb=fb, fc=fc, c=c:
                  self.w2t[e_, fb * 1024 + fc * 128: fb * 1024 + (fc + 1) * 128,
                           c * 512:(c + 1) * 512])

        def w2_cast(i, k):
            fc, c = k // 2, k % 2
            w2b = self.carve(self.o_w2b, 8 * 1024, BF16)
            _cast(w2_qd, i, k,
                  w2b[:, fc * 1024 + c * 512: fc * 1024 + (c + 1) * 512],
                  st_box["w2_mm_war"], w2c)

        def emit_rbc(e_):
            ep = e_ % 2
            for tt in range(4):
                tmm = self.mm(self.pE[:, 0:512],
                              self.er_sel_bf[0:E, e_ * 128:(e_ + 1) * 128],
                              self.router_bf[0:E, tt * 512:(tt + 1) * 512],
                              start=True, stop=True,
                              waits=[(self.sc_, self.router_sc),
                                     (self.sv_, pE_free_v),
                                     (self.sc_, pE_free_sc[0]),
                                     (self.sv_, rbc_war_sv[ep])],
                              inc=self.st_)
                cv = p.op("scalar", lambda e, tt=tt, ep=ep: e.copy(
                    self.rbc[:, ep * 2048 + tt * 512: ep * 2048 + tt * 512 + 512],
                    self.pE[:, 0:512]),
                    waits=[(self.st_, tmm)], inc=self.sc_)
                pE_free_sc[0] = cv
            rbc_sc[ep] = pE_free_sc[0]

        def emit_compute(i, slot_ops):
            e_, fb = blocks[i]
            ep = e_ % 2
            wpar = i % 2
            w1b = self.carve(self.o_w1b + wpar * 16384, 8 * 1024, BF16)
            w2b = self.carve(self.o_w2b, 8 * 1024, BF16)
            slot = 0
            for tt in range(4):
                for fc in range(8):
                    for f in slot_ops.get(slot, ()):
                        f()
                    slot += 1
                    bp = fc % 2
                    bank = (self.pA, self.pB)[bp]
                    for dk in range(DK):
                        self.mm(bank[:, 0:512],
                                w1b[:, dk * 1024 + fc * 128:
                                    dk * 1024 + fc * 128 + 128],
                                tgtfm[:, dk * 2048 + tt * 512:
                                      dk * 2048 + tt * 512 + 512],
                                start=(dk == 0), stop=(dk == DK - 1),
                                waits=([(self.sv_, w1c[i][0]),
                                        (self.sc_, w1c[i][1]),
                                        (self.sv_, self.tgtfm_sv),
                                        (self.sc_, bankAB_evac[bp])]
                                       if dk == 0 else ()),
                                inc=(self.st_ if dk == DK - 1 else None))
                    tmm = self.st_.n
                    hre = self.carve(self.o_hrelu + (fc % 2) * 1024, 512, BF16)
                    ev = p.op("scalar", lambda e, bank=bank, e_=e_, fb=fb,
                              fc=fc, hre=hre: e.activation(
                                  hre, bank[:, 0:512], AF.Relu,
                                  bias=self.b1_sb[:, e_ * 32 + fb * 8 + fc:
                                                  e_ * 32 + fb * 8 + fc + 1],
                                  scale=1.0),
                              waits=[(self.st_, tmm),
                                     (self.sv_, hrelu_war[fc % 2])]
                                    + kv_free,
                              inc=self.sc_)
                    bankAB_evac[bp] = ev
                    hm = p.op("vector", lambda e, hre=hre, fc=fc, tt=tt,
                              ep=ep, hblk=hblk: e.tensor_tensor(
                                  hblk[:, fc * 512:(fc + 1) * 512], hre,
                                  self.rbc[:, ep * 2048 + tt * 512:
                                           ep * 2048 + tt * 512 + 512],
                                  op=ALU.mult),
                              waits=[(self.sc_, ev), (self.sc_, rbc_sc[ep]),
                                     (self.st_, hblk_war[0])],
                              inc=self.sv_)
                    hrelu_war[fc % 2] = hm
                    h_last_sv[0] = hm
                st_box["w1_mm_war"][wpar] = self.st_.n
                for dk in range(DK):
                    bp = dk % 2
                    bank = (self.pC, self.pD)[bp]
                    for fc in range(8):
                        self.mm(bank[:, 0:512],
                                w2b[:, fc * 1024 + dk * 128:
                                    fc * 1024 + dk * 128 + 128],
                                hblk[:, fc * 512:(fc + 1) * 512],
                                start=(fc == 0), stop=(fc == 7),
                                waits=([(self.sv_, h_last_sv[0]),
                                        (self.sv_, w2c[i][0]),
                                        (self.sc_, w2c[i][1]),
                                        (self.sv_, bankCD_free[bp])]
                                       if fc == 0 else ()),
                                inc=(self.st_ if fc == 7 else None))
                    tmm = self.st_.n
                    av = p.op("vector", lambda e, bank=bank, dk=dk, tt=tt,
                              acc=acc: e.tensor_tensor(
                                  acc[:, dk * 2048 + tt * 512:
                                      dk * 2048 + tt * 512 + 512],
                                  bank[:, 0:512],
                                  acc[:, dk * 2048 + tt * 512:
                                      dk * 2048 + tt * 512 + 512],
                                  op=ALU.add),
                              waits=[(self.st_, tmm)], inc=self.sv_)
                    bankCD_free[bp] = av
                st_box["w2_mm_war"] = self.st_.n
                hblk_war[0] = self.st_.n
            if fb == 3:
                rbc_war_sv[ep] = h_last_sv[0]

        for k in range(16):
            if k >= 4:
                w1_cast(0, k - 4)
            w1_load(0, k)
        for k in range(12, 16):
            w1_cast(0, k)
        for k in range(16):
            if k >= 4:
                w2_cast(0, k - 4)
            w2_load(0, k)
        for k in range(12, 16):
            w2_cast(0, k)
        for i in range(NB):
            e_, fb = blocks[i]
            slot_ops = {}
            if i > 0:
                for j in range(8):
                    slot_ops.setdefault(j, []).append(
                        lambda j=j, i=i: w2_cast(i, 2 * j))
                    slot_ops.setdefault(j, []).append(
                        lambda j=j, i=i: w2_cast(i, 2 * j + 1))
                    if 2 * j + 4 < 16:
                        slot_ops.setdefault(j, []).append(
                            lambda j=j, i=i: w2_load(i, 2 * j + 4))
                        slot_ops.setdefault(j, []).append(
                            lambda j=j, i=i: w2_load(i, 2 * j + 5))
            if i + 1 < NB:
                for k in range(16):
                    if k >= 4:
                        slot_ops.setdefault(8 + k, []).append(
                            lambda k=k, i=i: w1_cast(i + 1, k - 4))
                    slot_ops.setdefault(8 + k, []).append(
                        lambda k=k, i=i: w1_load(i + 1, k))
                for k in range(12, 16):
                    slot_ops.setdefault(16 + k, []).append(
                        lambda k=k, i=i: w1_cast(i + 1, k))
                for c in range(4):
                    slot_ops.setdefault(28 + c, []).append(
                        lambda c=c, i=i: w2_load(i + 1, c))
            if fb == 0:
                emit_rbc(e_)
            emit_compute(i, slot_ops)
        self.stageF_end = dict(sv=self.sv_.n, sc=self.sc_.n, st=self.st_.n,
                               sg=self.sg_.n)

    def stage_o(self, s):
        p = self.p
        acc = self.carve(self.o_acc, 8 * 2048, F32)
        otm = self.carve(self.o_otm, 1024, F32)
        xo = self.carve(self.o_xo, 1024, F32)
        tmp = self.carve(self.o_tmp, 1024, F32)
        F_end = self.stageF_end
        pEF_free = [F_end["sc"], F_end["sc"]]
        acc_ready = [(self.sv_, F_end["sv"]), (self.sg_, F_end["sg"])]
        otm_war = 0
        otm_war_qs = 0
        xo_war = 0
        for g in range(NCH):
            cps = 0
            for dk in range(DK):
                bank = (self.pE, self.pF)[dk % 2]
                t1 = p.op("tensor", lambda e, bank=bank, dk=dk, g=g, acc=acc:
                          e.transpose(bank[:, 0:128],
                                      acc[:, dk * 2048 + g * 128:
                                          dk * 2048 + g * 128 + 128],
                                      self.identf[:]),
                          waits=acc_ready + [(self.sc_, pEF_free[dk % 2]),
                                             (self.sv_, otm_war),
                                             (self.qo_, otm_war_qs)],
                          inc=self.st_)
                cp = p.op("scalar", lambda e, bank=bank, dk=dk, otm=otm:
                          e.copy(otm[:, dk * 128:(dk + 1) * 128], bank[:, 0:128]),
                          waits=[(self.st_, t1)], inc=self.sc_)
                pEF_free[dk % 2] = cp
                cps = cp
            otm_f = lambda otm=otm: otm[:, 0:D]
            m, rstd, bdone = self.ln_stats(otm_f, D, [(self.sc_, cps)])
            a1 = p.op("vector", lambda e, otm=otm, m=m, rstd=rstd, tmp=tmp:
                      e.scalar_tensor_tensor(
                          tmp, otm, m, rstd.broadcast_to((128, D)),
                          op0=ALU.subtract, op1=ALU.mult),
                      waits=[(self.sc_, bdone)], inc=self.sv_)
            a2 = p.op("vector", lambda e, tmp=tmp: e.tensor_tensor(
                tmp, tmp, self.gam_b[:], op=ALU.mult),
                waits=[(self.sv_, a1)], inc=self.sv_)
            a3 = p.op("vector", lambda e, tmp=tmp: e.tensor_tensor(
                tmp, tmp, self.bet_b[:], op=ALU.add),
                waits=[(self.sv_, a2)], inc=self.sv_)
            tmp_f = lambda tmp=tmp: tmp[:, 0:D]
            m3, rstd3, b3done = self.ln_stats(tmp_f, D, [(self.sv_, a3)])
            qx = p.op("sync", lambda e, s=s, g=g, xo=xo: e.dma_start(
                xo, self.x1s[s, g * 128:(g + 1) * 128, :]),
                waits=[(self.sv_, xo_war), (self.qxd_, self.x1dump_qxd)],
                inc=self.qx_, inc_k=16)
            a4 = p.op("vector", lambda e, tmp=tmp, m3=m3, rstd3=rstd3, otm=otm:
                      e.scalar_tensor_tensor(
                          otm, tmp, m3, rstd3.broadcast_to((128, D)),
                          op0=ALU.subtract, op1=ALU.mult),
                      waits=[(self.sc_, b3done)], inc=self.sv_)
            a5 = p.op("vector", lambda e, otm=otm, xo=xo: e.tensor_tensor(
                otm, otm, xo, op=ALU.add),
                waits=[(self.sv_, a4), (self.qx_, qx)], inc=self.sv_)
            xo_war = a5
            otm_war_qs = p.op("sync", lambda e, s=s, g=g, otm=otm: e.dma_start(
                self.out[s, g * 128:(g + 1) * 128, :], otm),
                waits=[(self.sv_, a5)], inc=self.qo_, inc_k=16)
            otm_war = a5

    def seq_barrier(self):
        p = self.p
        marks = [(c, c.n) for c in (self.st_, self.sc_, self.sv_, self.sg_,
                 self.qs_, self.qx_, self.qd_, self.qq_, self.qxd_, self.qo_,
                 self.qk0_, self.qk1_, self.wv0_, self.wv1_, self.qf_,
                 self.qg_, self.qt0_, self.qt1_)]
        marks = [(c, v) for (c, v) in marks]
        for eng in ENGS:
            p.op(eng, lambda e: e.nop(), waits=marks)

    def finish(self):
        self.p.op("sync", lambda e: e.nop(), waits=[(self.qo_, self.qo_.n)])

    def build(self, stop_after=None, nseqs=SEQS):
        self.setup()
        self.seq_barrier()
        if stop_after == "S":
            self.finish()
            with self.nc.Block() as blk:
                self.p.replay(blk)
            return
        for s in range(nseqs):
            if s > 0:
                self.seq_barrier()
            self.stage_p(s)
            if stop_after == "P":
                break
            self.stage_a(s)
            if stop_after == "A":
                break
            self.stage_f(s)
            if stop_after == "F":
                break
            self.stage_o(s)
        self.finish()
        with self.nc.Block() as blk:
            self.p.replay(blk)


def build_program(stop_after=None, nseqs=SEQS):
    nc = bass.Bass("TRN2", target_bir_lowering=False)
    with contextlib.ExitStack() as st:
        b = Builder(nc, st)
        b.build(stop_after=stop_after, nseqs=nseqs)
    return nc


_NC_CACHE = None


def kernel(L0_attn_x, L0_attn_Wqk, L0_attn_bqk, L0_attn_Wv, L0_attn_bv,
           L0_ffn_Wr, L0_ffn_br, L0_ffn_W1, L0_ffn_b1, L0_ffn_W2, L0_ffn_b2,
           L0_ffn_gamma, L0_ffn_beta):
    global _NC_CACHE
    x = np.asarray(L0_attn_x, dtype=np.float32)
    Wqk = np.asarray(L0_attn_Wqk, np.float32).reshape(D, H, 2, DH)
    qpart = Wqk[:, :, 0, :].reshape(D, 8, 2 * DH)
    kpart = Wqk[:, :, 1, :].reshape(D, 8, 2 * DH)
    wqkp = np.ascontiguousarray(
        np.concatenate([qpart, kpart], axis=1).reshape(D, 2048))
    bqk = np.asarray(L0_attn_bqk, np.float32).reshape(H, 2, DH)
    bqkp = np.ascontiguousarray(np.concatenate(
        [bqk[:, 0, :].reshape(8, 128), bqk[:, 1, :].reshape(8, 128)],
        axis=0).reshape(2048))
    wv = np.ascontiguousarray(np.asarray(L0_attn_Wv, np.float32))
    bv = np.ascontiguousarray(np.asarray(L0_attn_bv, np.float32))
    wrt = np.ascontiguousarray(np.asarray(L0_ffn_Wr, np.float32).T)
    br = np.ascontiguousarray(np.asarray(L0_ffn_br, np.float32))
    w1t = np.ascontiguousarray(np.asarray(L0_ffn_W1, np.float32).transpose(0, 2, 1))
    b1p = np.ascontiguousarray(
        np.asarray(L0_ffn_b1, np.float32).reshape(E, 32, 128).transpose(2, 0, 1)
        .reshape(128, E * 32))
    w2t = np.ascontiguousarray(np.asarray(L0_ffn_W2, np.float32).transpose(0, 2, 1))
    b2 = np.ascontiguousarray(np.asarray(L0_ffn_b2, np.float32))
    gam = np.ascontiguousarray(np.asarray(L0_ffn_gamma, np.float32))
    bet = np.ascontiguousarray(np.asarray(L0_ffn_beta, np.float32))

    if _NC_CACHE is None:
        _NC_CACHE = build_program()
    nc = _NC_CACHE

    in_maps = []
    for core in range(8):
        shard = np.ascontiguousarray(x[2 * core: 2 * core + 2])
        in_maps.append(dict(x=shard, wqkp=wqkp, bqkp=bqkp, wv=wv, bv=bv,
                            wrt=wrt, br=br, w1t=w1t, b1p=b1p, w2t=w2t, b2=b2,
                            gam=gam, bet=bet))
    res = run_bass_kernel_spmd(nc, in_maps, core_ids=list(range(8)))
    out = np.concatenate([np.asarray(r["out"]).reshape(SEQS, S, D)
                          for r in res.results], axis=0)
    return out.astype(np.float32)
